# version 14
# baseline (speedup 1.0000x reference)
import os
import sys
sys.path.insert(0, "/opt/trn_rl_repo")
from contextlib import ExitStack

RT_TRUNC = float(os.environ.get("RT_TRUNC", "9"))

import numpy as np
import ml_dtypes

import concourse.bass as bass
import concourse.mybir as mybir
from concourse.bass_utils import run_bass_kernel_spmd

F32 = mybir.dt.float32
BF16 = mybir.dt.bfloat16
AF = mybir.ActivationFunctionType
ALU = mybir.AluOpType
AX = mybir.AxisListType
BF = ml_dtypes.bfloat16

D = 4096
S = 8192
NCORE = 8
BLK = 512
TPC = 1024
NKV = 8
GQ = 4
HD = 128
KVD = NKV * HD
NCH = D // 128
QSCALE = float(1.0 / np.sqrt(HD))
EXP_BIAS = -16.0
LN_EPS = 1e-5
NVIS = 17


class P:
    def __init__(self, nc, ctx):
        self.nc = nc
        self.q = {k: [] for k in ("sync", "tensor", "scalar", "vector", "gpsimd")}
        self.sems = {}
        self.res = {}
        self.ctx = ctx

    def sem(self, name):
        self.sems[name] = [self.ctx.enter_context(self.nc.semaphore(name)), 0]

    def tgt(self, name, n=1):
        s = self.sems[name]
        s[1] += n
        return s[0], s[1]

    def cnt(self, name):
        return self.sems[name][1]

    def hdl(self, name):
        return self.sems[name][0]

    def add(self, eng, fn):
        self.q[eng].append(fn)

    def consume(self, rname):
        w = self.res.get(rname, [])
        self.res[rname] = []
        return list(w)

    def mark(self, rname, sem, val):
        self.res.setdefault(rname, []).append((sem, val))

    def dma(self, qeng, sem, out_ap, in_ap, waits=(), wres=None, transpose=False):
        allw = list(waits)
        if wres:
            for rn in wres:
                allw += self.consume(rn)
        h, t = self.tgt(sem, 16)

        def f(E, out_ap=out_ap, in_ap=in_ap, allw=tuple(allw), h=h, tr=transpose):
            for ws, wv in allw:
                E.wait_ge(self.hdl(ws), wv)
            E.dma_start(out=out_ap, in_=in_ap, transpose=tr).then_inc(h, 16)

        self.add(qeng, f)
        return sem, t

    def step(self, eng, sem, body, waits=(), wres=None, n=1):
        allw = list(waits)
        if wres:
            for rn in wres:
                allw += self.consume(rn)
        h, t = self.tgt(sem, n)

        def f(E, body=body, allw=tuple(allw), h=h, n=n):
            for ws, wv in allw:
                E.wait_ge(self.hdl(ws), wv)
            body(E).then_inc(h, n)

        self.add(eng, f)
        return sem, t

    def entry(self, *sems):
        return [(s, self.cnt(s)) for s in sems if self.cnt(s) > 0]

    def st_totals(self):
        return self.entry("st0", "st1")

    def replay(self, block):
        for eng in ("sync", "tensor", "scalar", "vector", "gpsimd"):
            fns = self.q[eng]
            getattr(block, eng)(lambda E, fns=fns: [f(E) for f in fns])


def dap(t, off, dims):
    return bass.AP(t, off, dims)


def build_program(debug=None):
    nc = bass.Bass("TRN2", target_bir_lowering=False)

    x_loc = nc.dram_tensor("x_loc", [D, TPC], F32, kind="ExternalInput")
    wq4 = nc.dram_tensor("wq4", [NCH, 128, NCH, 128], F32, kind="ExternalInput")
    wk4 = nc.dram_tensor("wk4", [8, 128, NCH, 128], F32, kind="ExternalInput")
    wv4 = nc.dram_tensor("wv4", [8, 128, NCH, 128], F32, kind="ExternalInput")
    wo4 = nc.dram_tensor("wo4", [NCH, 128, NCH, 128], F32, kind="ExternalInput")
    bw4 = nc.dram_tensor("bw4", [NCH, 128, NCH, 128], F32, kind="ExternalInput")
    b24 = nc.dram_tensor("b24", [NCH, 128, 8, 128], F32, kind="ExternalInput")
    art4 = nc.dram_tensor("art4", [128, NCH * 80], F32, kind="ExternalInput")
    bias_t = nc.dram_tensor("bias_t", [128, NCH], F32, kind="ExternalInput")
    tab_t = nc.dram_tensor("tab_t", [128, 66], F32, kind="ExternalInput")
    dm_t = nc.dram_tensor("dm_t", [128, 4 * BLK], BF16, kind="ExternalInput")
    mask2_t = nc.dram_tensor("mask2_t", [16, 1024], BF16, kind="ExternalInput")
    ident_t = nc.dram_tensor("ident_t", [128, 128], F32, kind="ExternalInput")
    onesr_t = nc.dram_tensor("onesr_t", [1, 128], F32, kind="ExternalInput")
    onesc_t = nc.dram_tensor("onesc_t", [128, 1], F32, kind="ExternalInput")
    onesbc_t = nc.dram_tensor("onesbc_t", [128, 1], BF16, kind="ExternalInput")

    out_fm = nc.dram_tensor("out_fm", [D, TPC], F32, kind="ExternalOutput")

    kv_own = nc.dram_tensor("kv_own", [2 * KVD, TPC], BF16)
    kv_all = nc.dram_tensor("kv_all", [NCORE * 2 * KVD, TPC], BF16, addr_space="Shared")
    y1_park = nc.dram_tensor("y1_park", [D, TPC], F32)
    q_park = nc.dram_tensor("q_park", [D, TPC], BF16)
    x3_park = nc.dram_tensor("x3_park", [D, BLK], F32)
    rt_lg = nc.dram_tensor("rt_lg", [16, BLK], BF16)
    bwc = nc.dram_tensor("bwc", [NCH, 128, 40 * 128], BF16)
    woc = nc.dram_tensor("woc", [NCH, 128, NCH * 128], BF16)
    rt_cmb = nc.dram_tensor("rt_cmb", [4 * 128, 128], BF16)

    dbg = None
    if debug in ("y1",):
        dbg = nc.dram_tensor("dbg", [D, TPC], F32, kind="ExternalOutput")
    elif debug in ("ln1", "ln2"):
        dbg = nc.dram_tensor("dbg", [D, TPC], BF16, kind="ExternalOutput")
    elif debug == "kv":
        dbg = nc.dram_tensor("dbg", [2 * KVD, TPC], BF16, kind="ExternalOutput")
    elif debug == "q":
        dbg = nc.dram_tensor("dbg", [D, TPC], BF16, kind="ExternalOutput")
    elif debug == "x3":
        dbg = nc.dram_tensor("dbg", [D, BLK], F32, kind="ExternalOutput")
    elif debug == "cr":
        dbg = nc.dram_tensor("dbg", [128, 8 * BLK], BF16, kind="ExternalOutput")

    _LVL = {"ln1": 1, "cr": 1.5, "y1": 2, "ln2": 3, "kv": 4, "q": 5, "x3": 6}.get(debug, 99)

    with ExitStack() as ctx:
        e = ctx.enter_context
        p = P(nc, ctx)
        for s in ("ldc", "lw0", "lw1", "lx0", "lx1", "lk0", "lk1",
                  "lv0", "lv1", "lv2", "lv3", "lk2", "lk3", "lq", "st0", "st1", "cc", "rts", "rtl", "wc",
                  "te", "ac", "ve", "gp"):
            p.sem(s)

        ab = e(nc.sbuf_tensor("ab", [128, 58880], BF16))
        af = e(nc.sbuf_tensor("af", [128, 21504], F32))
        tab = e(nc.sbuf_tensor("tab", [128, 66], F32))
        dm = e(nc.sbuf_tensor("dm", [128, 4 * BLK], BF16))
        mask2 = e(nc.sbuf_tensor("mask2", [16, 1024], BF16))
        ident = e(nc.sbuf_tensor("ident", [128, 128], F32))
        onesr = e(nc.sbuf_tensor("onesr", [1, 128], F32))
        onesc = e(nc.sbuf_tensor("onesc", [128, 1], F32))
        onesbc = e(nc.sbuf_tensor("onesbc", [128, 1], BF16))
        biasb = e(nc.sbuf_tensor("biasb", [128, NCH], F32))
        lgsb = e(nc.sbuf_tensor("lgsb", [16, BLK], BF16))
        lgq = e(nc.sbuf_tensor("lgq", [128, 64], BF16))
        cmb16 = e(nc.sbuf_tensor("cmb16", [128, 64], BF16))

        WBF = [0, 5120]
        XNA = 10240
        ARTB = 43008
        CRA = 45568
        CR3 = 45568
        XBB = [53760, 54784]
        XSQ = [55808, 56832]
        KVQ = [57856, 58368]
        QSB = 0
        AOSB = 16384
        PSB = [32768, 34816, 36864, 38912]
        KSB = [40960, 41472, 41984, 42496]
        VTB = [49664, 50176, 50688, 51200]
        XN3 = 16384
        CBQ = 57856
        WST = [0, 5120]
        XCH = [10240, 12288]
        TEV = [14336, 14848]
        REV = [15360, 15872]
        OSB = [16384, 16896]
        RES2 = 17408
        RTR = 17920
        LNR = 18432
        RACC = [10240, 12288]
        BREC = 14336

        ps_a = e(nc.psum_tensor("ps_a", [128, 2048], F32))
        ps_b = e(nc.psum_tensor("ps_b", [128, 2048], F32))

        def abf(off, n):
            return ab[:, off:off + n]

        def aff(off, n):
            return af[:, off:off + n]

        def row(off, n):
            return af[0:1, off:off + n]

        p.dma("sync", "ldc", tab[:], tab_t[:, :])
        p.dma("sync", "ldc", dm[:], dm_t[:, :])
        p.dma("sync", "ldc", mask2[:], mask2_t[:, :])
        p.dma("sync", "ldc", ident[:], ident_t[:, :])
        p.dma("sync", "ldc", onesr[:], onesr_t[:, :])
        p.dma("sync", "ldc", onesc[:], onesc_t[:, :])
        p.dma("sync", "ldc", onesbc[:], onesbc_t[:, :])
        _, cld = p.dma("sync", "ldc", biasb[:], bias_t[:, :])
        p.step("vector", "ve", lambda v: v.tensor_scalar_mul(biasb[:], biasb[:], 0.5),
               waits=[("ldc", cld)])
        _, artld = p.dma("sync", "lw0", aff(WST[0], 2560), art4.ap())
        _, artcv = p.step("vector", "ve",
                          lambda v: v.tensor_copy(abf(ARTB, 2560), aff(WST[0], 2560)),
                          waits=[("lw0", artld)])
        p.mark("wst0", "ve", artcv)

        def emit_ln(src, xoff, extra_waits=(), chunk_waits=None):
            tsr, coff, pitch = src
            ssum = ps_a[0:1, 0:BLK]
            qsum = ps_a[0:1, BLK:2 * BLK]
            aps = ps_a[:, 2 * BLK:3 * BLK]
            bps = ps_a[:, 3 * BLK:4 * BLK]
            ew = p.entry("ve", "ac", "gp")

            def chunk2(ci, buf, extra):
                _, t16 = p.dma("sync", f"lx{buf}", aff(XCH[buf], 2 * BLK),
                               dap(tsr, (2 * ci) * 128 * pitch + coff,
                                   [[pitch, 128], [128 * pitch, 2], [1, BLK]]),
                               waits=extra, wres=[f"xch{buf}"])
                return aff(XCH[buf], 2 * BLK), [(f"lx{buf}", t16)]

            for ci in range(NCH // 2):
                buf = ci % 2
                cw = chunk_waits(ci) if chunk_waits else (list(extra_waits) if ci < 2 else [])
                ap_in, w = chunk2(ci, buf, cw)
                _, cv = p.step("vector", "ve",
                               lambda v, a=ap_in, b=buf: v.tensor_copy(abf(XBB[b], 2 * BLK), a),
                               waits=w, wres=[f"xbb{buf}"])
                _, sq = p.step("scalar", "ac",
                               lambda s, a=ap_in, b=buf: s.activation(abf(XSQ[b], 2 * BLK), a, AF.Square),
                               waits=w, wres=[f"xsq{buf}"])

                def mmfn(t, b=buf, ci=ci):
                    last = None
                    for k in range(2):
                        t.matmul(ssum, onesbc[:], abf(XBB[b] + k * BLK, BLK),
                                 start=(ci == 0 and k == 0), stop=(ci == NCH // 2 - 1 and k == 1))
                        last = t.matmul(qsum, onesbc[:], abf(XSQ[b] + k * BLK, BLK),
                                        start=(ci == 0 and k == 0), stop=(ci == NCH // 2 - 1 and k == 1))
                    return last
                _, mt = p.step("tensor", "te", mmfn,
                               waits=[("ve", cv), ("ac", sq)]
                                     + ((ew + [("ldc", p.cnt("ldc"))]) if ci == 0 else []))
                p.mark(f"xbb{buf}", "te", mt)
                p.mark(f"xsq{buf}", "te", mt)
                p.mark(f"xch{buf}", "te", mt)
            te_stats = p.cnt("te")
            mean_r, ex2_r, var_r, sd_r, al_r, be_r = (row(LNR + i * BLK, BLK) for i in range(6))

            wlnr = p.consume("lnr")
            _, s1_ = p.step("vector", "ve",
                            lambda v: v.tensor_scalar_mul(mean_r, ssum, 1.0 / D),
                            waits=[("te", te_stats)] + wlnr)
            _, s2_ = p.step("vector", "ve",
                            lambda v: v.tensor_scalar_mul(ex2_r, qsum, 1.0 / D),
                            waits=[("ve", s1_)])
            _, s3_ = p.step("vector", "ve",
                            lambda v: v.tensor_tensor(var_r, mean_r, mean_r, op=ALU.mult),
                            waits=[("ve", s2_)])
            _, s4_ = p.step("vector", "ve",
                            lambda v: v.tensor_sub(var_r, ex2_r, var_r),
                            waits=[("ve", s3_)])
            _, fv = p.step("vector", "ve",
                           lambda v: v.tensor_scalar_add(var_r, var_r, LN_EPS),
                           waits=[("ve", s4_)])
            _, fs = p.step("scalar", "ac", lambda s: s.activation(sd_r, var_r, AF.Sqrt),
                           waits=[("ve", fv)])
            _, r1_ = p.step("vector", "ve", lambda v: v.reciprocal(al_r, sd_r),
                            waits=[("ac", fs)])
            _, r2_ = p.step("vector", "ve",
                            lambda v: v.tensor_tensor(be_r, mean_r, al_r, op=ALU.mult),
                            waits=[("ve", r1_)])
            _, fv2 = p.step("vector", "ve",
                            lambda v: v.tensor_scalar_mul(be_r, be_r, -1.0),
                            waits=[("ve", r2_)])

            def bcfn(t):
                t.matmul(aps, onesr[:], al_r, start=True, stop=True)
                return t.matmul(bps, onesr[:], be_r, start=True, stop=True)
            _, bct = p.step("tensor", "te", bcfn, waits=[("ve", fv2)])
            p.mark("lnr", "te", bct)

            aps2 = dap(ps_a, 2 * BLK * 128 // 128 * 0 + 2 * BLK, [[2048, 128], [0, 2], [1, BLK]])
            bps2 = dap(ps_a, 3 * BLK, [[2048, 128], [0, 2], [1, BLK]])
            for ci in range(NCH // 2):
                buf = ci % 2
                ap_in, w = chunk2(ci, buf, ())
                wtv = p.consume(f"tevln{buf}")

                def appfn(v, a=ap_in, b=buf, ci=ci):
                    v.tensor_tensor(dap(af, XCH[b], [[21504, 128], [BLK, 2], [1, BLK]]) if False else a,
                                    a, aps2, op=ALU.mult)
                    return v.tensor_tensor(abf(xoff + 2 * ci * BLK, 2 * BLK), a, bps2, op=ALU.add)
                _, av = p.step("vector", "ve", appfn,
                               waits=w + [("te", bct)] + wtv)
                p.mark(f"xch{buf}", "ve", av)
            return p.cnt("ve")

        def emit_router(xoff, croff, xn_ready):
            arp = ps_b[0:80, 2 * BLK:3 * BLK]
            c2p = [ps_b[:, 2 * BLK:3 * BLK], ps_b[:, 3 * BLK:4 * BLK]]
            ew = p.entry("ve", "ac", "gp")
            if RT_TRUNC < 1:
                return p.cnt("ve")

            def armm(t):
                last = None
                for ic in range(NCH):
                    last = t.matmul(arp, abf(ARTB + ic * 80, 80),
                                    abf(xoff + ic * BLK, BLK),
                                    start=(ic == 0), stop=(ic == NCH - 1))
                return last
            _, art_t = p.step("tensor", "te", armm, waits=[("ve", xn_ready)] + ew)
            if RT_TRUNC < 1.5:
                return p.cnt("ve")

            res2_lo = af[0:64, RES2:RES2 + BLK]
            res2_hi = af[64:128, RES2:RES2 + BLK]

            def evfn(s):
                s.copy(lgsb[:], ps_b[64:80, 2 * BLK:3 * BLK])
                s.copy(res2_lo, ps_b[0:64, 2 * BLK:3 * BLK])
                return s.copy(res2_hi, ps_b[0:64, 2 * BLK:3 * BLK])
            _, ev = p.step("scalar", "ac", evfn, waits=[("te", art_t)],
                           wres=["res2", "lgsb"])
            if RT_TRUNC < 2:
                return p.cnt("ve")
            _, lgst = p.dma("gpsimd", "rts", rt_lg.ap(), lgsb[:],
                            waits=[("ac", ev)], wres=["rt_lg"])
            p.mark("lgsb", "rts", lgst)
            lql = None
            wlq = p.consume("lgq")
            for j in range(4):
                _, lql = p.dma("sync", "rtl", lgq[:, j * 16:(j + 1) * 16],
                               dap(rt_lg, j * 128, [[BLK, 16], [1, 128]]),
                               waits=[("rts", lgst)] + (wlq if j == 0 else []),
                               transpose=True)
            p.mark("rt_lg", "rtl", lql)
            if RT_TRUNC < 3:
                return p.cnt("ve")
            _, sm = p.step("scalar", "ac",
                           lambda s: s.activation(aff(RTR, 64), lgq[:], AF.Exp),
                           waits=[("rtl", lql)], wres=["rtr"])
            p.mark("lgq", "ac", sm)
            if RT_TRUNC < 4:
                return p.cnt("ve")

            g3 = lambda o: dap(af, RTR + o, [[21504, 128], [16, 4], [1, 16]])
            g1 = lambda o: dap(af, RTR + o, [[21504, 128], [1, 4]])
            g1b = lambda o: dap(af, RTR + o, [[21504, 128], [1, 4], [0, 16]])
            chain = [
                lambda v: v.tensor_reduce(g1(192), g3(0), axis=AX.X, op=ALU.add),
                lambda v: v.reciprocal(g1(196), g1(192)),
                lambda v: v.tensor_tensor(g3(0), g3(0), g1b(196), op=ALU.mult),
                lambda v: v.tensor_reduce(g1(200), g3(0), axis=AX.X, op=ALU.max),
                lambda v: v.tensor_tensor(g3(64), g3(0), g1b(200), op=ALU.is_ge),
                lambda v: v.tensor_tensor(g3(128), g3(0), g3(64), op=ALU.mult),
                lambda v: v.tensor_sub(g3(128), g3(0), g3(128)),
                lambda v: v.tensor_reduce(g1(204), g3(128), axis=AX.X, op=ALU.max),
                lambda v: v.tensor_tensor(g3(128), g3(128), g1b(204), op=ALU.is_ge),
                lambda v: v.tensor_tensor(g3(64), g3(64), g3(128), op=ALU.max),
                lambda v: v.tensor_tensor(g3(0), g3(0), g3(64), op=ALU.mult),
                lambda v: v.tensor_tensor(g1(200), g1(200), g1(204), op=ALU.add),
                lambda v: v.tensor_single_scalar(g1(200), g1(200), 1e-6, op=ALU.add),
                lambda v: v.reciprocal(g1(196), g1(200)),
                lambda v: v.tensor_tensor(g3(0), g3(0), g1b(196), op=ALU.mult),
            ]
            prev = [("ac", sm)]
            for fn in chain:
                _, t2 = p.step("vector", "ve", fn, waits=prev)
                prev = [("ve", t2)]
            wcm = p.consume("cmb16")
            _, cmv = p.step("vector", "ve",
                            lambda v: v.tensor_copy(cmb16[:], aff(RTR, 64)),
                            waits=prev + wcm)
            p.mark("rtr", "ve", cmv)
            if RT_TRUNC < 5:
                return p.cnt("ve")
            cst = None
            wrc = p.consume("rt_cmb")
            for g in range(4):
                _, cst = p.dma("gpsimd", "rts",
                               dap(rt_cmb, g * 128 * 128, [[128, 128], [1, 16]]),
                               cmb16[:, g * 16:(g + 1) * 16],
                               waits=[("ve", cmv)] + (wrc if g == 0 else []))
            p.mark("cmb16", "rts", cst)
            cbl = None
            wcb = p.consume("cbq") + p.consume("kvq0")
            for g in range(4):
                _, cbl = p.dma("sync", "rtl", abf(CBQ + g * 128, 128),
                               dap(rt_cmb, g * 128 * 128, [[128, 128], [1, 128]]),
                               waits=[("rts", cst)] + ((wcb + [("te", p.cnt("te"))]) if g == 0 else []),
                               transpose=True)
            p.mark("rt_cmb", "rtl", cbl)
            if RT_TRUNC < 6:
                return p.cnt("ve")
            cr_done = p.cnt("ve")
            for er in range(8):
                b = er % 2
                wr = p.consume(f"c2p{b}")

                def cbmm(t, er=er, b=b):
                    return t.matmul(c2p[b], mask2[:, er * 128:(er + 1) * 128],
                                    ab[0:16, CBQ:CBQ + BLK], start=True, stop=True)
                _, cbt = p.step("tensor", "te", cbmm, waits=[("rtl", cbl)] + wr)
                if er == 7:
                    p.mark("cbq", "te", cbt)
                    p.mark("kvq0", "te", cbt)

                def crfn(v, er=er, b=b):
                    return v.tensor_tensor(abf(croff + er * BLK, BLK), aff(RES2, BLK),
                                           c2p[b], op=ALU.mult)
                _, cr_done = p.step("vector", "ve", crfn, waits=[("te", cbt)])
                p.mark(f"c2p{b}", "ve", cr_done)
                p.mark("res2", "ve", cr_done) if er == 7 else None
            return cr_done

        def stream_w(oc, src4, wb, extra=None, cache=None):
            wr = p.consume(f"wst{wb}")
            sem = f"lw{wb}"
            _, l1 = p.dma("sync", sem, aff(WST[wb], NCH * 128),
                          dap(src4, oc * 128 * D, [[D, 128], [1, NCH * 128]]),
                          waits=wr)
            tot = NCH * 128
            if extra is not None:
                src4b, icntb, dstoff = extra
                _, l1 = p.dma("sync", sem, aff(WST[wb] + dstoff, icntb * 128),
                              dap(src4b, oc * 128 * (icntb * 128), [[icntb * 128, 128], [1, icntb * 128]]),
                              waits=wr)
                tot = dstoff + icntb * 128
            wrb = p.consume(f"wbf{wb}")
            half = (tot // 2) // 128 * 128
            _, cv = p.step("gpsimd", "gp",
                           lambda g, wb=wb, half=half: g.tensor_copy(
                               abf(WBF[wb], half), aff(WST[wb], half)),
                           waits=[(sem, l1)] + wrb)
            _, cv2 = p.step("vector", "ve",
                            lambda v, wb=wb, half=half, tot=tot: v.tensor_copy(
                                abf(WBF[wb] + half, tot - half), aff(WST[wb] + half, tot - half)),
                            waits=[(sem, l1)] + wrb)
            p.mark(f"wst{wb}", "gp", cv)
            p.mark(f"wst{wb}", "ve", cv2)
            if cache is not None:
                ct, ctot = cache
                _, cs = p.dma("gpsimd", "wc", dap(ct, oc * 128 * ctot, [[ctot, 128], [1, ctot]]),
                              abf(WBF[wb], ctot), waits=[("gp", cv), ("ve", cv2)])
                p.mark(f"wbf{wb}", "wc", cs)
            return [("gp", cv), ("ve", cv2)]

        def stream_wc(oc, csrc, wb, tot):
            wrb = p.consume(f"wbf{wb}")
            sem = f"lw{wb}"
            _, l1 = p.dma("sync", sem, abf(WBF[wb], tot),
                          dap(csrc, oc * 128 * tot, [[tot, 128], [1, tot]]),
                          waits=wrb + [("wc", p.cnt("wc"))])
            return [(sem, l1)]

        def emit_ffn(tiles, wmode="stream"):
            fps = [ps_b[:, 0:BLK], ps_b[:, BLK:2 * BLK]]
            ew = p.entry("ve", "ac", "gp")
            for oc in range(NCH):
                wb = oc % 2
                if wmode == "cache":
                    cv = stream_w(oc, bw4, wb, extra=(b24, 8, NCH * 128),
                                  cache=(bwc, 40 * 128))
                elif wmode == "cached":
                    cv = stream_wc(oc, bwc, wb, 40 * 128)
                else:
                    cv = stream_w(oc, bw4, wb, extra=(b24, 8, NCH * 128))
                for ti, tl in enumerate(tiles):
                    b = (oc * len(tiles) + ti) % 2
                    wfp = p.consume(f"fps{b}")

                    def mmfn(t, wb=wb, b=b, tl=tl):
                        last = None
                        for ic in range(NCH):
                            t.matmul(fps[b], abf(WBF[wb] + ic * 128, 128),
                                     abf(tl["xoff"] + ic * BLK, BLK),
                                     start=(ic == 0), stop=False)
                        for er in range(8):
                            last = t.matmul(fps[b], abf(WBF[wb] + NCH * 128 + er * 128, 128),
                                            abf(tl["croff"] + er * BLK, BLK),
                                            start=False, stop=(er == 7))
                        return last
                    _, mt = p.step("tensor", "te", mmfn,
                                   waits=cv + [("ve", tl["xn_ready"]),
                                               ("ve", tl["cr_ready"])] + wfp
                                         + (ew if oc == 0 and ti == 0 else []))
                    p.mark(f"wbf{wb}", "te", mt)
                    wtv = p.consume(f"tev{b}")
                    _, evv = p.step("vector", "ve",
                                    lambda v, b=b, oc=oc: v.tensor_scalar(
                                        aff(TEV[b], BLK), fps[b], 0.5,
                                        biasb[:, oc:oc + 1], op0=ALU.mult, op1=ALU.add),
                                    waits=[("te", mt)] + wtv)
                    p.mark(f"fps{b}", "ve", evv)
                    rsrc = tl["rsrc"]
                    wrv = p.consume(f"rev{b}")
                    _, rl = p.dma("sync", f"lx{b}", aff(REV[b], BLK),
                                  dap(rsrc[0], oc * 128 * rsrc[2] + rsrc[1], [[rsrc[2], 128], [1, BLK]]),
                                  waits=wrv)
                    wos = p.consume(f"osb{b}")

                    def addfn(v, b=b):
                        return v.tensor_add(aff(OSB[b], BLK), aff(REV[b], BLK), aff(TEV[b], BLK))
                    _, ad = p.step("vector", "ve", addfn,
                                   waits=[(f"lx{b}", rl)] + wos)
                    p.mark(f"tev{b}", "ve", ad)
                    p.mark(f"rev{b}", "ve", ad)
                    dst = tl["dst"]
                    _, st = p.dma("gpsimd", f"st{b}",
                                  dap(dst[0], oc * 128 * dst[2] + dst[1], [[dst[2], 128], [1, BLK]]),
                                  aff(OSB[b], BLK), waits=[("ve", ad)])
                    p.mark(f"osb{b}", f"st{b}", st)
                    if "log" in tl:
                        tl["log"][oc] = (f"st{b}", st)

        def emit_proj(src4, noc, xn_rdy2, scale, dst, dsem_unused, rowoff):
            fps = [ps_b[:, 0:BLK], ps_b[:, BLK:2 * BLK]]
            ew = p.entry("ve", "ac", "gp")
            for oc in range(noc):
                wb = oc % 2
                cv = stream_w(oc, src4, wb)
                for ti in range(2):
                    b = (oc * 2 + ti) % 2
                    wfp = p.consume(f"fps{b}")

                    def mmfn(t, wb=wb, b=b, ti=ti):
                        last = None
                        for ic in range(NCH):
                            last = t.matmul(fps[b], abf(WBF[wb] + ic * 128, 128),
                                            abf(XNA + ti * NCH * BLK + ic * BLK, BLK),
                                            start=(ic == 0), stop=(ic == NCH - 1))
                        return last
                    _, mt = p.step("tensor", "te", mmfn,
                                   waits=cv + [("ve", xn_rdy2[ti])] + wfp
                                         + (ew if oc == 0 and ti == 0 else []))
                    p.mark(f"wbf{wb}", "te", mt)
                    wkv = p.consume(f"kvq{b}")
                    if scale == 1.0:
                        _, evv = p.step("vector", "ve",
                                        lambda v, b=b: v.tensor_copy(abf(KVQ[b], BLK), fps[b]),
                                        waits=[("te", mt)] + wkv)
                    else:
                        _, evv = p.step("vector", "ve",
                                        lambda v, b=b: v.tensor_scalar_mul(abf(KVQ[b], BLK),
                                                                           fps[b], scale),
                                        waits=[("te", mt)] + wkv)
                    p.mark(f"fps{b}", "ve", evv)
                    _, st = p.dma("gpsimd", f"st{b}",
                                  dap(dst, (rowoff + oc * 128) * TPC + ti * BLK, [[TPC, 128], [1, BLK]]),
                                  abf(KVQ[b], BLK), waits=[("ve", evv)])
                    p.mark(f"kvq{b}", f"st{b}", st)

        xn1 = [emit_ln((x_loc, ti * BLK, TPC), XNA + ti * NCH * BLK) for ti in range(2)]
        if debug == "ln1":
            for ti in range(2):
                for chi in range(NCH):
                    p.dma("gpsimd", "st0",
                          dap(dbg, chi * 128 * TPC + ti * BLK, [[TPC, 128], [1, BLK]]),
                          abf(XNA + ti * NCH * BLK + chi * BLK, BLK),
                          waits=[("ve", xn1[1])])
        if _LVL >= 1.5:
            cr1 = [emit_router(XNA + ti * NCH * BLK, CRA + ti * 8 * BLK, xn1[ti]) for ti in range(2)]
        if debug == "cr":
            p.dma("gpsimd", "st0", dbg.ap(), abf(CRA, 8 * BLK),
                  waits=[("ve", cr1[0]), ("ve", cr1[1])])
        ffn1_logs = [{}, {}]
        if _LVL >= 2:
            emit_ffn([
                dict(xoff=XNA, croff=CRA, xn_ready=xn1[0], cr_ready=cr1[0],
                     rsrc=(x_loc, 0, TPC), dst=(y1_park, 0, TPC, None), log=ffn1_logs[0]),
                dict(xoff=XNA + NCH * BLK, croff=CRA + 8 * BLK, xn_ready=xn1[1], cr_ready=cr1[1],
                     rsrc=(x_loc, BLK, TPC), dst=(y1_park, BLK, TPC, None), log=ffn1_logs[1]),
            ], wmode="cache")
        sty_all = p.st_totals()
        if _LVL >= 3:
            xn2 = [emit_ln((y1_park, ti * BLK, TPC), XNA + ti * NCH * BLK,
                           chunk_waits=(lambda ci, ti=ti: [ffn1_logs[ti][2 * ci],
                                                           ffn1_logs[ti][2 * ci + 1]]))
                   for ti in range(2)]
            if debug == "ln2":
                for ti in range(2):
                    for chi in range(NCH):
                        p.dma("gpsimd", "st0",
                              dap(dbg, chi * 128 * TPC + ti * BLK, [[TPC, 128], [1, BLK]]),
                              abf(XNA + ti * NCH * BLK + chi * BLK, BLK),
                              waits=[("ve", xn2[1])])
        if _LVL >= 4:
            emit_proj(wk4, 8, xn2, 1.0, kv_own, "stk", 0)
            emit_proj(wv4, 8, xn2, 1.0, kv_own, "stk", KVD)
        stk_all = p.st_totals()
        if _LVL >= 5:
            _, ccv = p.step("gpsimd", "cc",
                            lambda g: g.collective_compute(
                                "AllGather", ALU.bypass,
                                replica_groups=[list(range(NCORE))],
                                ins=[kv_own.ap()], outs=[kv_all.ap()]),
                            waits=stk_all)
            emit_proj(wq4, NCH, xn2, QSCALE, q_park, "stq", 0)
        stq_all = p.st_totals()

        def emit_attention(qi):
            S_ps = ps_a
            AV_ps = ps_b
            nvis = 9 if qi == 0 else 17
            ewq = p.entry("te", "gp", "ve", "ac")
            qld = None
            for gd in range(4):
                _, qld = p.dma("sync", "lq", abf(QSB + gd * 4096, 4096),
                               dap(q_park, gd * 8 * 128 * TPC + qi * BLK,
                                   [[TPC, 128], [128 * TPC, 8], [1, BLK]]),
                               waits=stq_all + [("cc", ccv)] + (ewq if gd == 0 else []),
                               wres=["qsb"] if gd == 0 else None)
            rst = None
            evv = None
            for h in range(NKV):
                wrg = p.consume("raccg") + p.entry("te", "ve", "ac")
                _, msg = p.step("gpsimd", "gp",
                                lambda g: g.memset(aff(RACC[0], 4 * BLK), 0.0), waits=wrg)
                wrv = p.consume("raccv") + p.entry("te", "ac", "gp")
                _, msv = p.step("vector", "ve",
                                lambda v: v.memset(aff(RACC[1], 4 * BLK), 0.0), waits=wrv)
                gk = 0
                av_last = None
                pends = []
                for v in range(nvis):
                    par = v % 4
                    if v < nvis - 1:
                        kb = v
                        own = kb if kb < 8 else 15 - kb
                        slot = 0 if kb < 8 else 1
                        kvt = kv_all
                        krow = own * 2 * KVD + h * 128
                        vrow = own * 2 * KVD + KVD + h * 128
                        kcol = slot * BLK
                    else:
                        kvt = kv_own
                        krow = h * 128
                        vrow = KVD + h * 128
                        kcol = qi * BLK
                    _, kl = p.dma("sync", f"lk{par}", abf(KSB[par], BLK),
                                  dap(kvt, krow * TPC + kcol, [[TPC, 128], [1, BLK]]),
                                  wres=[f"ksb{par}"])
                    vl = None
                    wvs = p.consume(f"vtb{par}")
                    for kc in range(4):
                        _, vl = p.dma("sync", f"lv{par}", abf(VTB[par] + kc * 128, 128),
                                      dap(kvt, vrow * TPC + kcol + kc * 128, [[TPC, 128], [1, 128]]),
                                      waits=wvs if kc == 0 else (), transpose=True)
                    if v < nvis - 1:
                        idx = qi * 16 + kb
                        sc_ap = tab[:, 2 * idx:2 * idx + 1]
                        bi_ap = tab[:, 2 * idx + 1:2 * idx + 2]
                    else:
                        sc_ap = tab[:, 64:65]
                        bi_ap = tab[:, 65:66]
                    for kc in range(4):
                        pp = gk % 4
                        for hf in range(2):
                            ho = hf * 2
                            hoff = hf * 2 * BLK

                            def smm(t, par=par, kc=kc, h=h, ho=ho, hoff=hoff):
                                last = None
                                for g in (ho, ho + 1):
                                    last = t.matmul(S_ps[:, g * BLK:(g + 1) * BLK],
                                                    abf(KSB[par] + kc * 128, 128),
                                                    abf(QSB + (h * GQ + g) * BLK, BLK),
                                                    start=True, stop=True)
                                return last
                            sw = [(f"lk{par}", kl)] + p.consume(f"s_ps{hf}")
                            if v == 0 and h == 0 and kc == 0 and hf == 0:
                                sw.append(("lq", qld))
                            _, st_ = p.step("tensor", "te", smm, waits=sw)
                            if kc == 3 and hf == 1:
                                p.mark(f"ksb{par}", "te", st_)
                            if len(pends) >= 2:
                                av_last = pends.pop(0)()
                            wps = p.consume(f"psb{pp}_{hf}")

                            def expfn(s, pp=pp, hf=hf, hoff=hoff, sc=sc_ap, bi=bi_ap):
                                return s.activation(abf(PSB[pp] + hoff, 2 * BLK),
                                                    S_ps[:, hoff:hoff + 2 * BLK], AF.Exp,
                                                    bias=bi, scale=sc)
                            _, ex = p.step("scalar", "ac", expfn, waits=[("te", st_)] + wps)
                            p.mark(f"s_ps{hf}", "ac", ex)
                            pready = [("ac", ex)]
                            if v == nvis - 1:
                                def mskfn(vv, pp=pp, hoff=hoff, kc=kc):
                                    return vv.tensor_tensor(
                                        dap(ab, PSB[pp] + hoff, [[58880, 128], [BLK, 2], [1, BLK]]),
                                        dap(ab, PSB[pp] + hoff, [[58880, 128], [BLK, 2], [1, BLK]]),
                                        dap(dm, kc * BLK, [[4 * BLK, 128], [0, 2], [1, BLK]]),
                                        op=ALU.mult)
                                _, mk = p.step("vector", "ve", mskfn, waits=pready)
                                pready = [("ve", mk)]
                            if hf == 0:
                                exp_both = []
                            exp_both = exp_both + pready
                            if hf == 1:
                                on_gp = (gk % 5) < 2
                                eng = "gpsimd" if on_gp else "vector"
                                srw = "gp" if on_gp else "ve"
                                racc_o = RACC[0 if on_gp else 1]
                                waits_r = exp_both + pready[-1:] if False else exp_both + pready
                                if gk == 0:
                                    waits_r = waits_r + [("gp", msg)]
                                if gk == 2:
                                    waits_r = waits_r + [("ve", msv)]

                                def rafn(E, pp=pp, racc_o=racc_o):
                                    return E.tensor_tensor(
                                        aff(racc_o, 4 * BLK), aff(racc_o, 4 * BLK),
                                        abf(PSB[pp], 4 * BLK), op=ALU.add)
                                _, ra = p.step(eng, srw, rafn, waits=waits_r)
                                p.mark(f"psb{pp}_0", srw, ra)
                                p.mark(f"psb{pp}_1", srw, ra)

                            def mk_av(pready=pready, vl=vl, pp=pp, par=par, kc=kc, ho=ho,
                                      hf=hf, hoff=hoff,
                                      first=(v == 0 and kc == 0), last=(v == nvis - 1 and kc == 3)):
                                def avmm(t):
                                    li = None
                                    for g in (ho, ho + 1):
                                        li = t.matmul(AV_ps[:, g * BLK:(g + 1) * BLK],
                                                      abf(VTB[par] + kc * 128, 128),
                                                      abf(PSB[pp] + g * BLK, BLK),
                                                      start=first, stop=last)
                                    return li
                                aw = pready + [(f"lv{par}", vl)] + p.consume(f"av_ps{hf}")
                                _, avt = p.step("tensor", "te", avmm, waits=aw)
                                p.mark(f"psb{pp}_{hf}", "te", avt)
                                if kc == 3:
                                    p.mark(f"vtb{par}", "te", avt)
                                return avt
                            pends.append(mk_av)
                        gk += 1
                while pends:
                    av_last = pends.pop(0)()
                def rsmm(t):
                    last = None
                    for g in range(GQ):
                        t.matmul(S_ps[0:1, g * BLK:(g + 1) * BLK], onesc[:],
                                 aff(RACC[0] + g * BLK, BLK), start=True, stop=False)
                        last = t.matmul(S_ps[0:1, g * BLK:(g + 1) * BLK], onesc[:],
                                        aff(RACC[1] + g * BLK, BLK), start=False, stop=True)
                    return last
                _, rst = p.step("tensor", "te", rsmm,
                                waits=[("gp", p.cnt("gp")), ("ve", p.cnt("ve")),
                                       ("ac", p.cnt("ac"))] + p.consume("s_ps0") + p.consume("s_ps1"))
                p.mark("raccg", "te", rst)
                p.mark("raccv", "te", rst)
                _, rc = p.step("vector", "ve",
                               lambda v2: v2.reciprocal(row(LNR, 4 * BLK), S_ps[0:1, :]),
                               waits=[("te", rst)])

                def bcmm(t):
                    last = None
                    for g in range(GQ):
                        last = t.matmul(S_ps[:, g * BLK:(g + 1) * BLK], onesr[:],
                                        row(LNR + g * BLK, BLK), start=True, stop=True)
                    return last
                _, bct = p.step("tensor", "te", bcmm, waits=[("ve", rc)])
                wbr = p.consume("brec")
                _, bcv = p.step("scalar", "ac",
                                lambda s2: s2.copy(aff(BREC, 2048), ps_a[:, :]),
                                waits=[("te", bct)] + wbr)
                p.mark("s_ps0", "ac", bcv)
                p.mark("s_ps1", "ac", bcv)

                def evfn(v2, h=h):
                    for g in range(GQ - 1):
                        v2.tensor_tensor(abf(AOSB + (h * GQ + g) * BLK, BLK),
                                         AV_ps[:, g * BLK:(g + 1) * BLK],
                                         aff(BREC + g * BLK, BLK), op=ALU.mult)
                    return v2.tensor_tensor(abf(AOSB + (h * GQ + 3) * BLK, BLK),
                                            AV_ps[:, 3 * BLK:4 * BLK],
                                            aff(BREC + 3 * BLK, BLK), op=ALU.mult)
                _, evv = p.step("vector", "ve", evfn, waits=[("ac", bcv), ("te", av_last)])
                p.mark("av_ps0", "ve", evv)
                p.mark("av_ps1", "ve", evv)
                p.mark("brec", "ve", evv)
            for rn in ("xch0", "xch1", "tev0", "tev1", "rev0", "rev1", "osb0", "osb1"):
                p.mark(rn, "te", p.cnt("te"))
                p.mark(rn, "ve", p.cnt("ve"))
                p.mark(rn, "gp", p.cnt("gp"))
                p.mark(rn, "ac", p.cnt("ac"))
            p.mark("res2", "ve", p.cnt("ve"))
            p.mark("rtr", "ve", p.cnt("ve"))
            p.mark("lnr", "ve", p.cnt("ve"))
            for rn in ("wbf0", "wbf1"):
                p.mark(rn, "te", p.cnt("te"))
            p.mark("qsb", "te", p.cnt("te"))
            return p.cnt("ve")

        def emit_wo(qi, ao_ready, wo_log):
            wops = [ps_b[:, 0:BLK], ps_b[:, BLK:2 * BLK]]
            ew = p.entry("ve", "ac", "gp")
            for oc in range(NCH):
                wb = oc % 2
                if qi == 0:
                    cv = stream_w(oc, wo4, wb, cache=(woc, NCH * 128))
                else:
                    cv = stream_wc(oc, woc, wb, NCH * 128)
                b = oc % 2
                wfp = p.consume(f"fps{b}")

                def mmfn(t, wb=wb, b=b):
                    last = None
                    for ic in range(NCH):
                        last = t.matmul(wops[b], abf(WBF[wb] + ic * 128, 128),
                                        abf(AOSB + ic * BLK, BLK),
                                        start=(ic == 0), stop=(ic == NCH - 1))
                    return last
                _, mt = p.step("tensor", "te", mmfn,
                               waits=cv + [("ve", ao_ready)] + wfp
                                     + (ew if oc == 0 else []))
                p.mark(f"wbf{wb}", "te", mt)
                wtv = p.consume(f"tev{b}")
                _, evv = p.step("scalar", "ac",
                                lambda s, b=b: s.copy(aff(TEV[b], BLK), wops[b]),
                                waits=[("te", mt)] + wtv)
                p.mark(f"fps{b}", "ac", evv)
                wrv = p.consume(f"rev{b}")
                _, rl = p.dma("sync", f"lx{b}", aff(REV[b], BLK),
                              dap(y1_park, oc * 128 * TPC + qi * BLK, [[TPC, 128], [1, BLK]]),
                              waits=wrv)
                wos = p.consume(f"osb{b}")

                def addfn(v, b=b):
                    return v.tensor_add(aff(OSB[b], BLK), aff(REV[b], BLK), aff(TEV[b], BLK))
                _, ad = p.step("vector", "ve", addfn, waits=[("ac", evv), (f"lx{b}", rl)] + wos)
                p.mark(f"tev{b}", "ve", ad)
                p.mark(f"rev{b}", "ve", ad)
                _, st = p.dma("gpsimd", f"st{b}",
                              dap(x3_park, oc * 128 * BLK, [[BLK, 128], [1, BLK]]),
                              aff(OSB[b], BLK), waits=[("ve", ad)])
                p.mark(f"osb{b}", f"st{b}", st)
                wo_log[oc] = (f"st{b}", st)
            return p.st_totals()

        for qi in range(2 if _LVL >= 99 else (1 if _LVL >= 6 else 0)):
            ao_ready = emit_attention(qi)
            wo_log = {}
            x3_done = emit_wo(qi, ao_ready, wo_log)
            xn3 = emit_ln((x3_park, 0, BLK), XN3,
                          chunk_waits=(lambda ci: [wo_log[2 * ci], wo_log[2 * ci + 1]]))
            cr3 = emit_router(XN3, CR3, xn3)
            emit_ffn([
                dict(xoff=XN3, croff=CR3, xn_ready=xn3, cr_ready=cr3,
                     rsrc=(x3_park, 0, BLK), dst=(out_fm, qi * BLK, TPC, None)),
            ], wmode="cached")
            if debug == "x3" and qi == 0:
                p.dma("gpsimd", "st0", dbg.ap(), x3_park.ap(), waits=x3_done)
                break

        if debug == "y1":
            p.dma("gpsimd", "st0", dbg.ap(), y1_park.ap(), waits=sty_all)
        elif debug == "ln2":
            for ti in range(2):
                for chi in range(NCH):
                    p.dma("gpsimd", "st0",
                          dap(dbg, chi * 128 * TPC + ti * BLK, [[TPC, 128], [1, BLK]]),
                          abf(XNA + ti * NCH * BLK + chi * BLK, BLK),
                          waits=[("ve", xn2[1])])
        elif debug == "kv":
            p.dma("gpsimd", "st0", dbg.ap(), kv_own.ap(), waits=stk_all)
        elif debug == "q":
            p.dma("gpsimd", "st0", dbg.ap(), q_park.ap(), waits=stq_all)

        fin = p.st_totals()

        def finwait(g):
            for s, v in fin:
                g.wait_ge(p.hdl(s), v)
        p.add("gpsimd", finwait)

        with nc.Block() as block:
            p.replay(block)
    return nc


def _tile_oc(wt, oc_cnt):
    icc = wt.shape[0] // 128
    return np.ascontiguousarray(
        wt.reshape(icc, 128, oc_cnt, 128).transpose(2, 1, 0, 3))


def _consts():
    dm = np.zeros((128, 4 * BLK), np.float32)
    for kc in range(4):
        for pp in range(128):
            dm[pp, kc * BLK + kc * 128 + pp:kc * BLK + BLK] = 1.0
    mask2 = np.zeros((16, 1024), np.float32)
    for m in range(1024):
        mask2[m >> 6, m] = 0.5
    return dm.astype(BF), mask2.astype(BF)


def _tab_for_core(c):
    tab = np.zeros((128, 66), np.float32)
    tab[:, 64] = 1.0
    tab[:, 65] = EXP_BIAS
    for qi, gblk in ((0, c), (1, 15 - c)):
        for kb in range(16):
            idx = qi * 16 + kb
            if kb < gblk:
                tab[:, 2 * idx] = 1.0
                tab[:, 2 * idx + 1] = EXP_BIAS
            else:
                tab[:, 2 * idx] = 0.0
                tab[:, 2 * idx + 1] = -200.0
    return tab


def kernel(L0_attn_x, L0_attn_Wq, L0_attn_Wk, L0_attn_Wv, L0_attn_Wo,
           L0_ffn_base_w, L0_ffn_base_b, L0_ffn_A, L0_ffn_B, L0_ffn_router_w,
           _debug=None):
    x = np.asarray(L0_attn_x, np.float32).reshape(S, D)
    xT = np.ascontiguousarray(x.T)
    wq4 = _tile_oc(np.ascontiguousarray(np.asarray(L0_attn_Wq, np.float32).T), NCH)
    wk4 = _tile_oc(np.ascontiguousarray(np.asarray(L0_attn_Wk, np.float32).T), 8)
    wv4 = _tile_oc(np.ascontiguousarray(np.asarray(L0_attn_Wv, np.float32).T), 8)
    wo4 = _tile_oc(np.ascontiguousarray(np.asarray(L0_attn_Wo, np.float32).T), NCH)
    bw4 = _tile_oc(np.ascontiguousarray(np.asarray(L0_ffn_base_w, np.float32).T), NCH)
    b2 = np.ascontiguousarray(
        np.asarray(L0_ffn_B, np.float32).transpose(0, 2, 1).reshape(16 * 64, D))
    b24 = _tile_oc(b2, NCH)
    art = np.concatenate([np.asarray(L0_ffn_A, np.float32).T,
                          np.asarray(L0_ffn_router_w, np.float32).T], axis=1)
    art4 = np.ascontiguousarray(
        art.reshape(NCH, 128, 80).transpose(1, 0, 2).reshape(128, NCH * 80))
    bias_t = np.ascontiguousarray(np.asarray(L0_ffn_base_b, np.float32).reshape(NCH, 128).T)
    dmc, mask2c = _consts()
    ident = np.eye(128, dtype=np.float32)
    onesr = np.ones((1, 128), np.float32)
    onesc = np.ones((128, 1), np.float32)
    onesbc = np.ones((128, 1), np.float32).astype(BF)

    nc = build_program(debug=_debug)
    in_maps = []
    for c in range(NCORE):
        x_l = np.ascontiguousarray(np.concatenate(
            [xT[:, c * BLK:(c + 1) * BLK], xT[:, (15 - c) * BLK:(16 - c) * BLK]], axis=1))
        in_maps.append(dict(
            x_loc=x_l, wq4=wq4, wk4=wk4, wv4=wv4, wo4=wo4, bw4=bw4, b24=b24,
            art4=art4, bias_t=bias_t, tab_t=_tab_for_core(c), dm_t=dmc,
            mask2_t=mask2c, ident_t=ident, onesr_t=onesr, onesc_t=onesc,
            onesbc_t=onesbc))
    res = run_bass_kernel_spmd(nc, in_maps, core_ids=list(range(NCORE)))

    if _debug is not None:
        return res

    outT = np.empty((D, S), np.float32)
    for c in range(NCORE):
        sh = np.asarray(res.results[c]["out_fm"], np.float32)
        outT[:, c * BLK:(c + 1) * BLK] = sh[:, :BLK]
        outT[:, (15 - c) * BLK:(16 - c) * BLK] = sh[:, BLK:]
    return np.ascontiguousarray(outT.T).reshape(1, S, D)


if __name__ == "__main__":
    nc = build_program()
    n = sum(len(v) for v in nc.__dict__.get("_dummy", {}) or []) if False else None
    print("build ok")


# version 17
# speedup vs baseline: 1.0097x; 1.0097x over previous
import os
import sys
sys.path.insert(0, "/opt/trn_rl_repo")
from contextlib import ExitStack

RT_TRUNC = float(os.environ.get("RT_TRUNC", "9"))

import numpy as np
import ml_dtypes

import concourse.bass as bass
import concourse.mybir as mybir
from concourse.bass_utils import run_bass_kernel_spmd

F32 = mybir.dt.float32
BF16 = mybir.dt.bfloat16
AF = mybir.ActivationFunctionType
ALU = mybir.AluOpType
AX = mybir.AxisListType
BF = ml_dtypes.bfloat16

D = 4096
S = 8192
NCORE = 8
BLK = 512
TPC = 1024
NKV = 8
GQ = 4
HD = 128
KVD = NKV * HD
NCH = D // 128
QSCALE = float(1.0 / np.sqrt(HD))
EXP_BIAS = -16.0
LN_EPS = 1e-5
NVIS = 17


class P:
    def __init__(self, nc, ctx):
        self.nc = nc
        self.q = {k: [] for k in ("sync", "tensor", "scalar", "vector", "gpsimd")}
        self.sems = {}
        self.res = {}
        self.ctx = ctx

    def sem(self, name):
        self.sems[name] = [self.ctx.enter_context(self.nc.semaphore(name)), 0]

    def tgt(self, name, n=1):
        s = self.sems[name]
        s[1] += n
        return s[0], s[1]

    def cnt(self, name):
        return self.sems[name][1]

    def hdl(self, name):
        return self.sems[name][0]

    def add(self, eng, fn):
        self.q[eng].append(fn)

    def consume(self, rname):
        w = self.res.get(rname, [])
        self.res[rname] = []
        return list(w)

    def mark(self, rname, sem, val):
        self.res.setdefault(rname, []).append((sem, val))

    def dma(self, qeng, sem, out_ap, in_ap, waits=(), wres=None, transpose=False):
        allw = list(waits)
        if wres:
            for rn in wres:
                allw += self.consume(rn)
        h, t = self.tgt(sem, 16)

        def f(E, out_ap=out_ap, in_ap=in_ap, allw=tuple(allw), h=h, tr=transpose):
            for ws, wv in allw:
                E.wait_ge(self.hdl(ws), wv)
            E.dma_start(out=out_ap, in_=in_ap, transpose=tr).then_inc(h, 16)

        self.add(qeng, f)
        return sem, t

    def step(self, eng, sem, body, waits=(), wres=None, n=1):
        allw = list(waits)
        if wres:
            for rn in wres:
                allw += self.consume(rn)
        h, t = self.tgt(sem, n)

        def f(E, body=body, allw=tuple(allw), h=h, n=n):
            for ws, wv in allw:
                E.wait_ge(self.hdl(ws), wv)
            body(E).then_inc(h, n)

        self.add(eng, f)
        return sem, t

    def entry(self, *sems):
        return [(s, self.cnt(s)) for s in sems if self.cnt(s) > 0]

    def st_totals(self):
        return self.entry("st0", "st1")

    def replay(self, block):
        for eng in ("sync", "tensor", "scalar", "vector", "gpsimd"):
            fns = self.q[eng]
            getattr(block, eng)(lambda E, fns=fns: [f(E) for f in fns])


def dap(t, off, dims):
    return bass.AP(t, off, dims)


def build_program(debug=None):
    nc = bass.Bass("TRN2", target_bir_lowering=False)

    x_loc = nc.dram_tensor("x_loc", [D, TPC], F32, kind="ExternalInput")
    wq4 = nc.dram_tensor("wq4", [NCH, 128, NCH, 128], F32, kind="ExternalInput")
    wk4 = nc.dram_tensor("wk4", [8, 128, NCH, 128], F32, kind="ExternalInput")
    wv4 = nc.dram_tensor("wv4", [8, 128, NCH, 128], F32, kind="ExternalInput")
    wo4 = nc.dram_tensor("wo4", [NCH, 128, NCH, 128], F32, kind="ExternalInput")
    bw4 = nc.dram_tensor("bw4", [NCH, 128, NCH, 128], F32, kind="ExternalInput")
    b24 = nc.dram_tensor("b24", [NCH, 128, 8, 128], F32, kind="ExternalInput")
    art4 = nc.dram_tensor("art4", [128, NCH * 80], F32, kind="ExternalInput")
    bias_t = nc.dram_tensor("bias_t", [128, NCH], F32, kind="ExternalInput")
    tab_t = nc.dram_tensor("tab_t", [128, 66], F32, kind="ExternalInput")
    dm_t = nc.dram_tensor("dm_t", [128, 4 * BLK], BF16, kind="ExternalInput")
    mask2_t = nc.dram_tensor("mask2_t", [16, 1024], BF16, kind="ExternalInput")
    ident_t = nc.dram_tensor("ident_t", [128, 128], F32, kind="ExternalInput")
    onesr_t = nc.dram_tensor("onesr_t", [1, 128], F32, kind="ExternalInput")
    onesc_t = nc.dram_tensor("onesc_t", [128, 1], F32, kind="ExternalInput")
    onesbc_t = nc.dram_tensor("onesbc_t", [128, 1], BF16, kind="ExternalInput")

    out_fm = nc.dram_tensor("out_fm", [D, TPC], F32, kind="ExternalOutput")

    kv_own = nc.dram_tensor("kv_own", [2 * KVD, TPC], BF16)
    kv_all = nc.dram_tensor("kv_all", [NCORE * 2 * KVD, TPC], BF16, addr_space="Shared")
    y1_park = nc.dram_tensor("y1_park", [D, TPC], F32)
    q_park = nc.dram_tensor("q_park", [D, TPC], BF16)
    x3_park = nc.dram_tensor("x3_park", [D, BLK], F32)
    rt_lg = nc.dram_tensor("rt_lg", [16, BLK], BF16)
    bwc = nc.dram_tensor("bwc", [NCH, 128, 40 * 128], BF16)
    woc = nc.dram_tensor("woc", [NCH, 128, NCH * 128], BF16)
    rt_cmb = nc.dram_tensor("rt_cmb", [4 * 128, 128], BF16)

    dbg = None
    if debug in ("y1",):
        dbg = nc.dram_tensor("dbg", [D, TPC], F32, kind="ExternalOutput")
    elif debug in ("ln1", "ln2"):
        dbg = nc.dram_tensor("dbg", [D, TPC], BF16, kind="ExternalOutput")
    elif debug == "kv":
        dbg = nc.dram_tensor("dbg", [2 * KVD, TPC], BF16, kind="ExternalOutput")
    elif debug == "q":
        dbg = nc.dram_tensor("dbg", [D, TPC], BF16, kind="ExternalOutput")
    elif debug == "x3":
        dbg = nc.dram_tensor("dbg", [D, BLK], F32, kind="ExternalOutput")
    elif debug == "cr":
        dbg = nc.dram_tensor("dbg", [128, 8 * BLK], BF16, kind="ExternalOutput")

    _LVL = {"ln1": 1, "cr": 1.5, "y1": 2, "ln2": 3, "kv": 4, "q": 5, "x3": 6}.get(debug, 99)

    with ExitStack() as ctx:
        e = ctx.enter_context
        p = P(nc, ctx)
        for s in ("ldc", "lw0", "lw1", "lx0", "lx1", "lk0", "lk1",
                  "lv0", "lv1", "lv2", "lv3", "lk2", "lk3", "lq", "st0", "st1", "cc", "rts", "rtl", "wc",
                  "te", "ac", "ve", "gp"):
            p.sem(s)

        ab = e(nc.sbuf_tensor("ab", [128, 58880], BF16))
        af = e(nc.sbuf_tensor("af", [128, 21504], F32))
        tab = e(nc.sbuf_tensor("tab", [128, 66], F32))
        dm = e(nc.sbuf_tensor("dm", [128, 4 * BLK], BF16))
        mask2 = e(nc.sbuf_tensor("mask2", [16, 1024], BF16))
        ident = e(nc.sbuf_tensor("ident", [128, 128], F32))
        onesr = e(nc.sbuf_tensor("onesr", [1, 128], F32))
        onesc = e(nc.sbuf_tensor("onesc", [128, 1], F32))
        onesbc = e(nc.sbuf_tensor("onesbc", [128, 1], BF16))
        biasb = e(nc.sbuf_tensor("biasb", [128, NCH], F32))
        lgsb = e(nc.sbuf_tensor("lgsb", [16, BLK], BF16))
        lgq = e(nc.sbuf_tensor("lgq", [128, 64], BF16))
        cmb16 = e(nc.sbuf_tensor("cmb16", [128, 64], BF16))

        WBF = [0, 5120]
        XNA = 10240
        ARTB = 43008
        CRA = 45568
        CR3 = 45568
        XBB = [53760, 54784]
        XSQ = [55808, 56832]
        KVQ = [57856, 58368]
        QSB = 0
        AOSB = 16384
        PSB = [32768, 34816, 36864, 38912]
        KSB = [40960, 41472, 41984, 42496]
        VTB = [49664, 50176, 50688, 51200]
        XN3 = 16384
        CBQ = 57856
        WST = [0, 5120]
        XCH = [10240, 12288]
        TEV = [14336, 14848]
        REV = [15360, 15872]
        OSB = [16384, 16896]
        RES2 = 17408
        RTR = 17920
        LNR = 18432
        RACC = [10240, 12288]
        BREC = 14336

        ps_a = e(nc.psum_tensor("ps_a", [128, 2048], F32))
        ps_b = e(nc.psum_tensor("ps_b", [128, 2048], F32))

        def abf(off, n):
            return ab[:, off:off + n]

        def aff(off, n):
            return af[:, off:off + n]

        def row(off, n):
            return af[0:1, off:off + n]

        p.dma("sync", "ldc", tab[:], tab_t[:, :])
        p.dma("sync", "ldc", dm[:], dm_t[:, :])
        p.dma("sync", "ldc", mask2[:], mask2_t[:, :])
        p.dma("sync", "ldc", ident[:], ident_t[:, :])
        p.dma("sync", "ldc", onesr[:], onesr_t[:, :])
        p.dma("sync", "ldc", onesc[:], onesc_t[:, :])
        p.dma("sync", "ldc", onesbc[:], onesbc_t[:, :])
        _, cld = p.dma("sync", "ldc", biasb[:], bias_t[:, :])
        p.step("vector", "ve", lambda v: v.tensor_scalar_mul(biasb[:], biasb[:], 0.5),
               waits=[("ldc", cld)])
        _, artld = p.dma("sync", "lw0", aff(WST[0], 2560), art4.ap())
        _, artcv = p.step("vector", "ve",
                          lambda v: v.tensor_copy(abf(ARTB, 2560), aff(WST[0], 2560)),
                          waits=[("lw0", artld)])
        p.mark("wst0", "ve", artcv)

        def emit_ln(src, xoff, extra_waits=(), chunk_waits=None, pre=None, offs=(0, BLK)):
            tsr, coff, pitch = src
            if pre is None:
                s_off, q_off = offs
            else:
                s_off, q_off = pre[0], pre[1]
            ssum = ps_a[0:1, s_off:s_off + BLK]
            qsum = ps_a[0:1, q_off:q_off + BLK]
            aps = ps_a[:, s_off:s_off + BLK]
            bps = ps_a[:, q_off:q_off + BLK]
            ew = p.entry("ve", "ac", "gp")

            def chunk2(ci, buf, extra):
                _, t16 = p.dma("sync", f"lx{buf}", aff(XCH[buf], 2 * BLK),
                               dap(tsr, (2 * ci) * 128 * pitch + coff,
                                   [[pitch, 128], [128 * pitch, 2], [1, BLK]]),
                               waits=extra, wres=[f"xch{buf}"])
                return aff(XCH[buf], 2 * BLK), [(f"lx{buf}", t16)]

            for ci in range(NCH // 2) if pre is None else ():
                buf = ci % 2
                cw = chunk_waits(ci) if chunk_waits else (list(extra_waits) if ci < 2 else [])
                ap_in, w = chunk2(ci, buf, cw)
                _, cv = p.step("vector", "ve",
                               lambda v, a=ap_in, b=buf: v.tensor_copy(abf(XBB[b], 2 * BLK), a),
                               waits=w, wres=[f"xbb{buf}"])
                _, sq = p.step("scalar", "ac",
                               lambda s, a=ap_in, b=buf: s.activation(abf(XSQ[b], 2 * BLK), a, AF.Square),
                               waits=w, wres=[f"xsq{buf}"])

                def mmfn(t, b=buf, ci=ci):
                    last = None
                    for k in range(2):
                        t.matmul(ssum, onesbc[:], abf(XBB[b] + k * BLK, BLK),
                                 start=(ci == 0 and k == 0), stop=(ci == NCH // 2 - 1 and k == 1))
                        last = t.matmul(qsum, onesbc[:], abf(XSQ[b] + k * BLK, BLK),
                                        start=(ci == 0 and k == 0), stop=(ci == NCH // 2 - 1 and k == 1))
                    return last
                _, mt = p.step("tensor", "te", mmfn,
                               waits=[("ve", cv), ("ac", sq)]
                                     + ((ew + [("ldc", p.cnt("ldc"))]) if ci == 0 else []))
                p.mark(f"xbb{buf}", "te", mt)
                p.mark(f"xsq{buf}", "te", mt)
                p.mark(f"xch{buf}", "te", mt)
            te_stats = pre[2] if pre is not None else p.cnt("te")
            mean_r, ex2_r, var_r, sd_r, al_r, be_r = (row(LNR + i * BLK, BLK) for i in range(6))

            wlnr = p.consume("lnr")
            _, s1_ = p.step("vector", "ve",
                            lambda v: v.tensor_scalar_mul(mean_r, ssum, 1.0 / D),
                            waits=[("te", te_stats)] + wlnr)
            _, s2_ = p.step("vector", "ve",
                            lambda v: v.tensor_scalar_mul(ex2_r, qsum, 1.0 / D),
                            waits=[("ve", s1_)])
            _, s3_ = p.step("vector", "ve",
                            lambda v: v.tensor_tensor(var_r, mean_r, mean_r, op=ALU.mult),
                            waits=[("ve", s2_)])
            _, s4_ = p.step("vector", "ve",
                            lambda v: v.tensor_sub(var_r, ex2_r, var_r),
                            waits=[("ve", s3_)])
            _, fv = p.step("vector", "ve",
                           lambda v: v.tensor_scalar_add(var_r, var_r, LN_EPS),
                           waits=[("ve", s4_)])
            _, fs = p.step("scalar", "ac", lambda s: s.activation(sd_r, var_r, AF.Sqrt),
                           waits=[("ve", fv)])
            _, r1_ = p.step("vector", "ve", lambda v: v.reciprocal(al_r, sd_r),
                            waits=[("ac", fs)])
            _, r2_ = p.step("vector", "ve",
                            lambda v: v.tensor_tensor(be_r, mean_r, al_r, op=ALU.mult),
                            waits=[("ve", r1_)])
            _, fv2 = p.step("vector", "ve",
                            lambda v: v.tensor_scalar_mul(be_r, be_r, -1.0),
                            waits=[("ve", r2_)])

            def bcfn(t):
                t.matmul(aps, onesr[:], al_r, start=True, stop=True)
                return t.matmul(bps, onesr[:], be_r, start=True, stop=True)
            _, bct = p.step("tensor", "te", bcfn, waits=[("ve", fv2)])
            p.mark("lnr", "te", bct)

            aps2 = dap(ps_a, s_off, [[2048, 128], [0, 2], [1, BLK]])
            bps2 = dap(ps_a, q_off, [[2048, 128], [0, 2], [1, BLK]])
            for ci in range(NCH // 2):
                buf = ci % 2
                ap_in, w = chunk2(ci, buf, chunk_waits(ci) if (pre is not None and chunk_waits) else ())
                wtv = p.consume(f"tevln{buf}")

                def appfn(v, a=ap_in, b=buf, ci=ci):
                    v.tensor_tensor(dap(af, XCH[b], [[21504, 128], [BLK, 2], [1, BLK]]) if False else a,
                                    a, aps2, op=ALU.mult)
                    return v.tensor_tensor(abf(xoff + 2 * ci * BLK, 2 * BLK), a, bps2, op=ALU.add)
                _, av = p.step("vector", "ve", appfn,
                               waits=w + [("te", bct)] + wtv)
                p.mark(f"xch{buf}", "ve", av)
            return p.cnt("ve")

        def emit_router(xoff, croff, xn_ready):
            arp = ps_b[0:80, 2 * BLK:3 * BLK]
            c2p = [ps_b[:, 2 * BLK:3 * BLK], ps_b[:, 3 * BLK:4 * BLK]]
            ew = p.entry("ve", "ac", "gp")
            if RT_TRUNC < 1:
                return p.cnt("ve")

            def armm(t):
                last = None
                for ic in range(NCH):
                    last = t.matmul(arp, abf(ARTB + ic * 80, 80),
                                    abf(xoff + ic * BLK, BLK),
                                    start=(ic == 0), stop=(ic == NCH - 1))
                return last
            _, art_t = p.step("tensor", "te", armm, waits=[("ve", xn_ready)] + ew)
            if RT_TRUNC < 1.5:
                return p.cnt("ve")

            res2_lo = af[0:64, RES2:RES2 + BLK]
            res2_hi = af[64:128, RES2:RES2 + BLK]

            def evfn(s):
                s.copy(lgsb[:], ps_b[64:80, 2 * BLK:3 * BLK])
                s.copy(res2_lo, ps_b[0:64, 2 * BLK:3 * BLK])
                return s.copy(res2_hi, ps_b[0:64, 2 * BLK:3 * BLK])
            _, ev = p.step("scalar", "ac", evfn, waits=[("te", art_t)],
                           wres=["res2", "lgsb"])
            if RT_TRUNC < 2:
                return p.cnt("ve")
            _, lgst = p.dma("gpsimd", "rts", rt_lg.ap(), lgsb[:],
                            waits=[("ac", ev)], wres=["rt_lg"])
            p.mark("lgsb", "rts", lgst)
            lql = None
            wlq = p.consume("lgq")
            for j in range(4):
                _, lql = p.dma("sync", "rtl", lgq[:, j * 16:(j + 1) * 16],
                               dap(rt_lg, j * 128, [[BLK, 16], [1, 128]]),
                               waits=[("rts", lgst)] + (wlq if j == 0 else []),
                               transpose=True)
            p.mark("rt_lg", "rtl", lql)
            if RT_TRUNC < 3:
                return p.cnt("ve")
            _, sm = p.step("scalar", "ac",
                           lambda s: s.activation(aff(RTR, 64), lgq[:], AF.Exp),
                           waits=[("rtl", lql)], wres=["rtr"])
            p.mark("lgq", "ac", sm)
            if RT_TRUNC < 4:
                return p.cnt("ve")

            g3 = lambda o: dap(af, RTR + o, [[21504, 128], [16, 4], [1, 16]])
            g1 = lambda o: dap(af, RTR + o, [[21504, 128], [1, 4]])
            g1b = lambda o: dap(af, RTR + o, [[21504, 128], [1, 4], [0, 16]])
            chain = [
                lambda v: v.tensor_reduce(g1(192), g3(0), axis=AX.X, op=ALU.add),
                lambda v: v.reciprocal(g1(196), g1(192)),
                lambda v: v.tensor_tensor(g3(0), g3(0), g1b(196), op=ALU.mult),
                lambda v: v.tensor_reduce(g1(200), g3(0), axis=AX.X, op=ALU.max),
                lambda v: v.tensor_tensor(g3(64), g3(0), g1b(200), op=ALU.is_ge),
                lambda v: v.tensor_tensor(g3(128), g3(0), g3(64), op=ALU.mult),
                lambda v: v.tensor_sub(g3(128), g3(0), g3(128)),
                lambda v: v.tensor_reduce(g1(204), g3(128), axis=AX.X, op=ALU.max),
                lambda v: v.tensor_tensor(g3(128), g3(128), g1b(204), op=ALU.is_ge),
                lambda v: v.tensor_tensor(g3(64), g3(64), g3(128), op=ALU.max),
                lambda v: v.tensor_tensor(g3(0), g3(0), g3(64), op=ALU.mult),
                lambda v: v.tensor_tensor(g1(200), g1(200), g1(204), op=ALU.add),
                lambda v: v.tensor_single_scalar(g1(200), g1(200), 1e-6, op=ALU.add),
                lambda v: v.reciprocal(g1(196), g1(200)),
                lambda v: v.tensor_tensor(g3(0), g3(0), g1b(196), op=ALU.mult),
            ]
            prev = [("ac", sm)]
            for fn in chain:
                _, t2 = p.step("vector", "ve", fn, waits=prev)
                prev = [("ve", t2)]
            wcm = p.consume("cmb16")
            _, cmv = p.step("vector", "ve",
                            lambda v: v.tensor_copy(cmb16[:], aff(RTR, 64)),
                            waits=prev + wcm)
            p.mark("rtr", "ve", cmv)
            if RT_TRUNC < 5:
                return p.cnt("ve")
            cst = None
            wrc = p.consume("rt_cmb")
            for g in range(4):
                _, cst = p.dma("gpsimd", "rts",
                               dap(rt_cmb, g * 128 * 128, [[128, 128], [1, 16]]),
                               cmb16[:, g * 16:(g + 1) * 16],
                               waits=[("ve", cmv)] + (wrc if g == 0 else []))
            p.mark("cmb16", "rts", cst)
            cbl = None
            wcb = p.consume("cbq") + p.consume("kvq0")
            for g in range(4):
                _, cbl = p.dma("sync", "rtl", abf(CBQ + g * 128, 128),
                               dap(rt_cmb, g * 128 * 128, [[128, 128], [1, 128]]),
                               waits=[("rts", cst)] + ((wcb + [("te", p.cnt("te"))]) if g == 0 else []),
                               transpose=True)
            p.mark("rt_cmb", "rtl", cbl)
            if RT_TRUNC < 6:
                return p.cnt("ve")
            cr_done = p.cnt("ve")
            for er in range(8):
                b = er % 2
                wr = p.consume(f"c2p{b}")

                def cbmm(t, er=er, b=b):
                    return t.matmul(c2p[b], mask2[:, er * 128:(er + 1) * 128],
                                    ab[0:16, CBQ:CBQ + BLK], start=True, stop=True)
                _, cbt = p.step("tensor", "te", cbmm, waits=[("rtl", cbl)] + wr)
                if er == 7:
                    p.mark("cbq", "te", cbt)
                    p.mark("kvq0", "te", cbt)

                def crfn(v, er=er, b=b):
                    return v.tensor_tensor(abf(croff + er * BLK, BLK), aff(RES2, BLK),
                                           c2p[b], op=ALU.mult)
                _, cr_done = p.step("vector", "ve", crfn, waits=[("te", cbt)])
                p.mark(f"c2p{b}", "ve", cr_done)
                p.mark("res2", "ve", cr_done) if er == 7 else None
            return cr_done

        def stream_w(oc, src4, wb, extra=None, cache=None):
            wr = p.consume(f"wst{wb}")
            sem = f"lw{wb}"
            _, l1 = p.dma("sync", sem, aff(WST[wb], NCH * 128),
                          dap(src4, oc * 128 * D, [[D, 128], [1, NCH * 128]]),
                          waits=wr)
            tot = NCH * 128
            if extra is not None:
                src4b, icntb, dstoff = extra
                _, l1 = p.dma("sync", sem, aff(WST[wb] + dstoff, icntb * 128),
                              dap(src4b, oc * 128 * (icntb * 128), [[icntb * 128, 128], [1, icntb * 128]]),
                              waits=wr)
                tot = dstoff + icntb * 128
            wrb = p.consume(f"wbf{wb}")
            half = (tot // 2) // 128 * 128
            _, cv = p.step("gpsimd", "gp",
                           lambda g, wb=wb, half=half: g.tensor_copy(
                               abf(WBF[wb], half), aff(WST[wb], half)),
                           waits=[(sem, l1)] + wrb)
            _, cv2 = p.step("vector", "ve",
                            lambda v, wb=wb, half=half, tot=tot: v.tensor_copy(
                                abf(WBF[wb] + half, tot - half), aff(WST[wb] + half, tot - half)),
                            waits=[(sem, l1)] + wrb)
            p.mark(f"wst{wb}", "gp", cv)
            p.mark(f"wst{wb}", "ve", cv2)
            if cache is not None:
                ct, ctot = cache
                _, cs = p.dma("gpsimd", "wc", dap(ct, oc * 128 * ctot, [[ctot, 128], [1, ctot]]),
                              abf(WBF[wb], ctot), waits=[("gp", cv), ("ve", cv2)])
                p.mark(f"wbf{wb}", "wc", cs)
            return [("gp", cv), ("ve", cv2)]

        def stream_wc(oc, csrc, wb, tot):
            wrb = p.consume(f"wbf{wb}")
            sem = f"lw{wb}"
            _, l1 = p.dma("sync", sem, abf(WBF[wb], tot),
                          dap(csrc, oc * 128 * tot, [[tot, 128], [1, tot]]),
                          waits=wrb + [("wc", p.cnt("wc"))])
            return [(sem, l1)]

        def emit_ffn(tiles, wmode="stream"):
            fps = [ps_b[:, 0:BLK], ps_b[:, BLK:2 * BLK]]
            ew = p.entry("ve", "ac", "gp")
            for oc in range(NCH):
                wb = oc % 2
                if wmode == "cache":
                    cv = stream_w(oc, bw4, wb, extra=(b24, 8, NCH * 128),
                                  cache=(bwc, 40 * 128))
                elif wmode == "cached":
                    cv = stream_wc(oc, bwc, wb, 40 * 128)
                else:
                    cv = stream_w(oc, bw4, wb, extra=(b24, 8, NCH * 128))
                for ti, tl in enumerate(tiles):
                    b = (oc * len(tiles) + ti) % 2
                    wfp = p.consume(f"fps{b}")

                    def mmfn(t, wb=wb, b=b, tl=tl):
                        last = None
                        for ic in range(NCH):
                            t.matmul(fps[b], abf(WBF[wb] + ic * 128, 128),
                                     abf(tl["xoff"] + ic * BLK, BLK),
                                     start=(ic == 0), stop=False)
                        for er in range(8):
                            last = t.matmul(fps[b], abf(WBF[wb] + NCH * 128 + er * 128, 128),
                                            abf(tl["croff"] + er * BLK, BLK),
                                            start=False, stop=(er == 7))
                        return last
                    _, mt = p.step("tensor", "te", mmfn,
                                   waits=cv + [("ve", tl["xn_ready"]),
                                               ("ve", tl["cr_ready"])] + wfp
                                         + (ew if oc == 0 and ti == 0 else []))
                    p.mark(f"wbf{wb}", "te", mt)
                    wtv = p.consume(f"tev{b}")
                    _, evv = p.step("vector", "ve",
                                    lambda v, b=b, oc=oc: v.tensor_scalar(
                                        aff(TEV[b], BLK), fps[b], 0.5,
                                        biasb[:, oc:oc + 1], op0=ALU.mult, op1=ALU.add),
                                    waits=[("te", mt)] + wtv)
                    p.mark(f"fps{b}", "ve", evv)
                    rsrc = tl["rsrc"]
                    wrv = p.consume(f"rev{b}")
                    _, rl = p.dma("sync", f"lx{b}", aff(REV[b], BLK),
                                  dap(rsrc[0], oc * 128 * rsrc[2] + rsrc[1], [[rsrc[2], 128], [1, BLK]]),
                                  waits=wrv)
                    wos = p.consume(f"osb{b}")

                    def addfn(v, b=b):
                        return v.tensor_add(aff(OSB[b], BLK), aff(REV[b], BLK), aff(TEV[b], BLK))
                    _, ad = p.step("vector", "ve", addfn,
                                   waits=[(f"lx{b}", rl)] + wos)
                    p.mark(f"tev{b}", "ve", ad)
                    p.mark(f"rev{b}", "ve", ad)
                    dst = tl["dst"]
                    _, st = p.dma("gpsimd", f"st{b}",
                                  dap(dst[0], oc * 128 * dst[2] + dst[1], [[dst[2], 128], [1, BLK]]),
                                  aff(OSB[b], BLK), waits=[("ve", ad)])
                    p.mark(f"osb{b}", f"st{b}", st)
                    if "log" in tl:
                        tl["log"][oc] = (f"st{b}", st)
                    if "stat_bank" in tl:
                        sb_s, sb_q = tl["stat_bank"]
                        wxb = p.consume(f"fxb{b}")
                        _, scv = p.step("vector", "ve",
                                        lambda v, b=b: v.tensor_copy(abf(XBB[b], BLK), aff(OSB[b], BLK)),
                                        waits=wxb)
                        p.mark(f"osb{b}", "ve", scv)
                        wxq = p.consume(f"fxq{b}")
                        _, ssq = p.step("scalar", "ac",
                                        lambda s, b=b: s.activation(abf(XSQ[b], BLK), aff(OSB[b], BLK), AF.Square),
                                        waits=[("ve", ad)] + wxq)
                        p.mark(f"osb{b}", "ac", ssq)

                        def stmm(t, b=b, oc=oc, sb_s=sb_s, sb_q=sb_q, nt=len(tiles)):
                            t.matmul(sb_s, onesbc[:], abf(XBB[b], BLK),
                                     start=(oc == 0), stop=(oc == NCH - 1))
                            return t.matmul(sb_q, onesbc[:], abf(XSQ[b], BLK),
                                            start=(oc == 0), stop=(oc == NCH - 1))
                        _, smt = p.step("tensor", "te", stmm, waits=[("ve", scv), ("ac", ssq)])
                        p.mark(f"fxb{b}", "te", smt)
                        p.mark(f"fxq{b}", "te", smt)
                        tl["stat_done"] = smt

        def emit_proj(src4, noc, xn_rdy2, scale, dst, dsem_unused, rowoff):
            fps = [ps_b[:, 0:BLK], ps_b[:, BLK:2 * BLK]]
            ew = p.entry("ve", "ac", "gp")
            for oc in range(noc):
                wb = oc % 2
                cv = stream_w(oc, src4, wb)
                for ti in range(2):
                    b = (oc * 2 + ti) % 2
                    wfp = p.consume(f"fps{b}")

                    def mmfn(t, wb=wb, b=b, ti=ti):
                        last = None
                        for ic in range(NCH):
                            last = t.matmul(fps[b], abf(WBF[wb] + ic * 128, 128),
                                            abf(XNA + ti * NCH * BLK + ic * BLK, BLK),
                                            start=(ic == 0), stop=(ic == NCH - 1))
                        return last
                    _, mt = p.step("tensor", "te", mmfn,
                                   waits=cv + [("ve", xn_rdy2[ti])] + wfp
                                         + (ew if oc == 0 and ti == 0 else []))
                    p.mark(f"wbf{wb}", "te", mt)
                    wkv = p.consume(f"kvq{b}")
                    if scale == 1.0:
                        _, evv = p.step("vector", "ve",
                                        lambda v, b=b: v.tensor_copy(abf(KVQ[b], BLK), fps[b]),
                                        waits=[("te", mt)] + wkv)
                    else:
                        _, evv = p.step("vector", "ve",
                                        lambda v, b=b: v.tensor_scalar_mul(abf(KVQ[b], BLK),
                                                                           fps[b], scale),
                                        waits=[("te", mt)] + wkv)
                    p.mark(f"fps{b}", "ve", evv)
                    _, st = p.dma("gpsimd", f"st{b}",
                                  dap(dst, (rowoff + oc * 128) * TPC + ti * BLK, [[TPC, 128], [1, BLK]]),
                                  abf(KVQ[b], BLK), waits=[("ve", evv)])
                    p.mark(f"kvq{b}", f"st{b}", st)

        xn1 = [emit_ln((x_loc, ti * BLK, TPC), XNA + ti * NCH * BLK,
                       offs=(0, BLK) if ti == 0 else (2 * BLK, 3 * BLK)) for ti in range(2)]
        if debug == "ln1":
            for ti in range(2):
                for chi in range(NCH):
                    p.dma("gpsimd", "st0",
                          dap(dbg, chi * 128 * TPC + ti * BLK, [[TPC, 128], [1, BLK]]),
                          abf(XNA + ti * NCH * BLK + chi * BLK, BLK),
                          waits=[("ve", xn1[1])])
        if _LVL >= 1.5:
            cr1 = [emit_router(XNA + ti * NCH * BLK, CRA + ti * 8 * BLK, xn1[ti]) for ti in range(2)]
        if debug == "cr":
            p.dma("gpsimd", "st0", dbg.ap(), abf(CRA, 8 * BLK),
                  waits=[("ve", cr1[0]), ("ve", cr1[1])])
        ffn1_logs = [{}, {}]
        ffn1_tiles = [
            dict(xoff=XNA, croff=CRA, xn_ready=xn1[0] if _LVL >= 2 else 0, cr_ready=(cr1[0] if _LVL >= 2 else 0),
                 rsrc=(x_loc, 0, TPC), dst=(y1_park, 0, TPC, None), log=ffn1_logs[0],
                 stat_bank=(ps_a[0:1, 0:BLK], ps_a[0:1, BLK:2 * BLK]), stat_off=(0, BLK)),
            dict(xoff=XNA + NCH * BLK, croff=CRA + 8 * BLK, xn_ready=(xn1[1] if _LVL >= 2 else 0),
                 cr_ready=(cr1[1] if _LVL >= 2 else 0),
                 rsrc=(x_loc, BLK, TPC), dst=(y1_park, BLK, TPC, None), log=ffn1_logs[1],
                 stat_bank=(ps_a[0:1, 2 * BLK:3 * BLK], ps_a[0:1, 3 * BLK:4 * BLK]),
                 stat_off=(2 * BLK, 3 * BLK)),
        ]
        if _LVL >= 2:
            for _t in ffn1_tiles:
                _t["xn_ready"] = xn1[0] if _t is ffn1_tiles[0] else xn1[1]
                _t["cr_ready"] = cr1[0] if _t is ffn1_tiles[0] else cr1[1]
            emit_ffn(ffn1_tiles, wmode="cache")
        sty_all = p.st_totals()
        if _LVL >= 3:
            xn2 = [emit_ln((y1_park, ti * BLK, TPC), XNA + ti * NCH * BLK,
                           chunk_waits=(lambda ci, ti=ti: [ffn1_logs[ti][2 * ci],
                                                           ffn1_logs[ti][2 * ci + 1]]),
                           pre=(ffn1_tiles[ti]["stat_off"][0], ffn1_tiles[ti]["stat_off"][1],
                                ffn1_tiles[ti]["stat_done"]))
                   for ti in range(2)]
            if debug == "ln2":
                for ti in range(2):
                    for chi in range(NCH):
                        p.dma("gpsimd", "st0",
                              dap(dbg, chi * 128 * TPC + ti * BLK, [[TPC, 128], [1, BLK]]),
                              abf(XNA + ti * NCH * BLK + chi * BLK, BLK),
                              waits=[("ve", xn2[1])])
        if _LVL >= 4:
            emit_proj(wk4, 8, xn2, 1.0, kv_own, "stk", 0)
            emit_proj(wv4, 8, xn2, 1.0, kv_own, "stk", KVD)
        stk_all = p.st_totals()
        if _LVL >= 5:
            _, ccv = p.step("gpsimd", "cc",
                            lambda g: g.collective_compute(
                                "AllGather", ALU.bypass,
                                replica_groups=[list(range(NCORE))],
                                ins=[kv_own.ap()], outs=[kv_all.ap()]),
                            waits=stk_all)
            emit_proj(wq4, NCH, xn2, QSCALE, q_park, "stq", 0)
        stq_all = p.st_totals()

        def emit_attention(qi):
            S_ps = ps_a
            AV_ps = ps_b
            nvis = 9 if qi == 0 else 17
            ewq = p.entry("te", "gp", "ve", "ac")
            qld = None
            for gd in range(4):
                _, qld = p.dma("sync", "lq", abf(QSB + gd * 4096, 4096),
                               dap(q_park, gd * 8 * 128 * TPC + qi * BLK,
                                   [[TPC, 128], [128 * TPC, 8], [1, BLK]]),
                               waits=stq_all + [("cc", ccv)] + (ewq if gd == 0 else []),
                               wres=["qsb"] if gd == 0 else None)
            rst = None
            evv = None
            for h in range(NKV):
                wrg = p.consume("raccg") + p.entry("te", "ve", "ac")
                _, msg = p.step("gpsimd", "gp",
                                lambda g: g.memset(aff(RACC[0], 4 * BLK), 0.0), waits=wrg)
                wrv = p.consume("raccv") + p.entry("te", "ac", "gp")
                _, msv = p.step("vector", "ve",
                                lambda v: v.memset(aff(RACC[1], 4 * BLK), 0.0), waits=wrv)
                gk = 0
                av_last = None
                pends = []
                for v in range(nvis):
                    par = v % 4
                    if v < nvis - 1:
                        kb = v
                        own = kb if kb < 8 else 15 - kb
                        slot = 0 if kb < 8 else 1
                        kvt = kv_all
                        krow = own * 2 * KVD + h * 128
                        vrow = own * 2 * KVD + KVD + h * 128
                        kcol = slot * BLK
                    else:
                        kvt = kv_own
                        krow = h * 128
                        vrow = KVD + h * 128
                        kcol = qi * BLK
                    _, kl = p.dma("sync", f"lk{par}", abf(KSB[par], BLK),
                                  dap(kvt, krow * TPC + kcol, [[TPC, 128], [1, BLK]]),
                                  wres=[f"ksb{par}"])
                    vl = None
                    wvs = p.consume(f"vtb{par}")
                    for kc in range(4):
                        _, vl = p.dma("sync", f"lv{par}", abf(VTB[par] + kc * 128, 128),
                                      dap(kvt, vrow * TPC + kcol + kc * 128, [[TPC, 128], [1, 128]]),
                                      waits=wvs if kc == 0 else (), transpose=True)
                    if v < nvis - 1:
                        idx = qi * 16 + kb
                        sc_ap = tab[:, 2 * idx:2 * idx + 1]
                        bi_ap = tab[:, 2 * idx + 1:2 * idx + 2]
                    else:
                        sc_ap = tab[:, 64:65]
                        bi_ap = tab[:, 65:66]
                    for kc in range(4):
                        pp = gk % 4
                        for hf in range(2):
                            ho = hf * 2
                            hoff = hf * 2 * BLK

                            def smm(t, par=par, kc=kc, h=h, ho=ho, hoff=hoff):
                                last = None
                                for g in (ho, ho + 1):
                                    last = t.matmul(S_ps[:, g * BLK:(g + 1) * BLK],
                                                    abf(KSB[par] + kc * 128, 128),
                                                    abf(QSB + (h * GQ + g) * BLK, BLK),
                                                    start=True, stop=True)
                                return last
                            sw = [(f"lk{par}", kl)] + p.consume(f"s_ps{hf}")
                            if v == 0 and h == 0 and kc == 0 and hf == 0:
                                sw.append(("lq", qld))
                            _, st_ = p.step("tensor", "te", smm, waits=sw)
                            if kc == 3 and hf == 1:
                                p.mark(f"ksb{par}", "te", st_)
                            if len(pends) >= 2:
                                av_last = pends.pop(0)()
                            wps = p.consume(f"psb{pp}_{hf}")

                            def expfn(s, pp=pp, hf=hf, hoff=hoff, sc=sc_ap, bi=bi_ap):
                                return s.activation(abf(PSB[pp] + hoff, 2 * BLK),
                                                    S_ps[:, hoff:hoff + 2 * BLK], AF.Exp,
                                                    bias=bi, scale=sc)
                            _, ex = p.step("scalar", "ac", expfn, waits=[("te", st_)] + wps)
                            p.mark(f"s_ps{hf}", "ac", ex)
                            pready = [("ac", ex)]
                            if v == nvis - 1:
                                def mskfn(vv, pp=pp, hoff=hoff, kc=kc):
                                    return vv.tensor_tensor(
                                        dap(ab, PSB[pp] + hoff, [[58880, 128], [BLK, 2], [1, BLK]]),
                                        dap(ab, PSB[pp] + hoff, [[58880, 128], [BLK, 2], [1, BLK]]),
                                        dap(dm, kc * BLK, [[4 * BLK, 128], [0, 2], [1, BLK]]),
                                        op=ALU.mult)
                                _, mk = p.step("vector", "ve", mskfn, waits=pready)
                                pready = [("ve", mk)]
                            if hf == 0:
                                exp_both = []
                            exp_both = exp_both + pready
                            if hf == 1:
                                on_gp = (gk % 5) < 2
                                eng = "gpsimd" if on_gp else "vector"
                                srw = "gp" if on_gp else "ve"
                                racc_o = RACC[0 if on_gp else 1]
                                waits_r = exp_both + pready[-1:] if False else exp_both + pready
                                if gk == 0:
                                    waits_r = waits_r + [("gp", msg)]
                                if gk == 2:
                                    waits_r = waits_r + [("ve", msv)]

                                def rafn(E, pp=pp, racc_o=racc_o):
                                    return E.tensor_tensor(
                                        aff(racc_o, 4 * BLK), aff(racc_o, 4 * BLK),
                                        abf(PSB[pp], 4 * BLK), op=ALU.add)
                                _, ra = p.step(eng, srw, rafn, waits=waits_r)
                                p.mark(f"psb{pp}_0", srw, ra)
                                p.mark(f"psb{pp}_1", srw, ra)

                            def mk_av(pready=pready, vl=vl, pp=pp, par=par, kc=kc, ho=ho,
                                      hf=hf, hoff=hoff,
                                      first=(v == 0 and kc == 0), last=(v == nvis - 1 and kc == 3)):
                                def avmm(t):
                                    li = None
                                    for g in (ho, ho + 1):
                                        li = t.matmul(AV_ps[:, g * BLK:(g + 1) * BLK],
                                                      abf(VTB[par] + kc * 128, 128),
                                                      abf(PSB[pp] + g * BLK, BLK),
                                                      start=first, stop=last)
                                    return li
                                aw = pready + [(f"lv{par}", vl)] + p.consume(f"av_ps{hf}")
                                _, avt = p.step("tensor", "te", avmm, waits=aw)
                                p.mark(f"psb{pp}_{hf}", "te", avt)
                                if kc == 3:
                                    p.mark(f"vtb{par}", "te", avt)
                                return avt
                            pends.append(mk_av)
                        gk += 1
                while pends:
                    av_last = pends.pop(0)()
                def rsmm(t):
                    last = None
                    for g in range(GQ):
                        t.matmul(S_ps[0:1, g * BLK:(g + 1) * BLK], onesc[:],
                                 aff(RACC[0] + g * BLK, BLK), start=True, stop=False)
                        last = t.matmul(S_ps[0:1, g * BLK:(g + 1) * BLK], onesc[:],
                                        aff(RACC[1] + g * BLK, BLK), start=False, stop=True)
                    return last
                _, rst = p.step("tensor", "te", rsmm,
                                waits=[("gp", p.cnt("gp")), ("ve", p.cnt("ve")),
                                       ("ac", p.cnt("ac"))] + p.consume("s_ps0") + p.consume("s_ps1"))
                p.mark("raccg", "te", rst)
                p.mark("raccv", "te", rst)
                _, rc = p.step("vector", "ve",
                               lambda v2: v2.reciprocal(row(LNR, 4 * BLK), S_ps[0:1, :]),
                               waits=[("te", rst)])

                def bcmm(t):
                    last = None
                    for g in range(GQ):
                        last = t.matmul(S_ps[:, g * BLK:(g + 1) * BLK], onesr[:],
                                        row(LNR + g * BLK, BLK), start=True, stop=True)
                    return last
                _, bct = p.step("tensor", "te", bcmm, waits=[("ve", rc)])
                wbr = p.consume("brec")
                _, bcv = p.step("scalar", "ac",
                                lambda s2: s2.copy(aff(BREC, 2048), ps_a[:, :]),
                                waits=[("te", bct)] + wbr)
                p.mark("s_ps0", "ac", bcv)
                p.mark("s_ps1", "ac", bcv)

                def evfn(v2, h=h):
                    for g in range(GQ - 1):
                        v2.tensor_tensor(abf(AOSB + (h * GQ + g) * BLK, BLK),
                                         AV_ps[:, g * BLK:(g + 1) * BLK],
                                         aff(BREC + g * BLK, BLK), op=ALU.mult)
                    return v2.tensor_tensor(abf(AOSB + (h * GQ + 3) * BLK, BLK),
                                            AV_ps[:, 3 * BLK:4 * BLK],
                                            aff(BREC + 3 * BLK, BLK), op=ALU.mult)
                _, evv = p.step("vector", "ve", evfn, waits=[("ac", bcv), ("te", av_last)])
                p.mark("av_ps0", "ve", evv)
                p.mark("av_ps1", "ve", evv)
                p.mark("brec", "ve", evv)
            for rn in ("xch0", "xch1", "tev0", "tev1", "rev0", "rev1", "osb0", "osb1"):
                p.mark(rn, "te", p.cnt("te"))
                p.mark(rn, "ve", p.cnt("ve"))
                p.mark(rn, "gp", p.cnt("gp"))
                p.mark(rn, "ac", p.cnt("ac"))
            p.mark("res2", "ve", p.cnt("ve"))
            p.mark("rtr", "ve", p.cnt("ve"))
            p.mark("lnr", "ve", p.cnt("ve"))
            for rn in ("wbf0", "wbf1"):
                p.mark(rn, "te", p.cnt("te"))
            p.mark("qsb", "te", p.cnt("te"))
            return p.cnt("ve")

        def emit_wo(qi, ao_ready, wo_log, wo_stat):
            wops = [ps_b[:, 0:BLK], ps_b[:, BLK:2 * BLK]]
            ew = p.entry("ve", "ac", "gp")
            for oc in range(NCH):
                wb = oc % 2
                if qi == 0:
                    cv = stream_w(oc, wo4, wb, cache=(woc, NCH * 128))
                else:
                    cv = stream_wc(oc, woc, wb, NCH * 128)
                b = oc % 2
                wfp = p.consume(f"fps{b}")

                def mmfn(t, wb=wb, b=b):
                    last = None
                    for ic in range(NCH):
                        last = t.matmul(wops[b], abf(WBF[wb] + ic * 128, 128),
                                        abf(AOSB + ic * BLK, BLK),
                                        start=(ic == 0), stop=(ic == NCH - 1))
                    return last
                _, mt = p.step("tensor", "te", mmfn,
                               waits=cv + [("ve", ao_ready)] + wfp
                                     + (ew if oc == 0 else []))
                p.mark(f"wbf{wb}", "te", mt)
                wtv = p.consume(f"tev{b}")
                _, evv = p.step("scalar", "ac",
                                lambda s, b=b: s.copy(aff(TEV[b], BLK), wops[b]),
                                waits=[("te", mt)] + wtv)
                p.mark(f"fps{b}", "ac", evv)
                wrv = p.consume(f"rev{b}")
                _, rl = p.dma("sync", f"lx{b}", aff(REV[b], BLK),
                              dap(y1_park, oc * 128 * TPC + qi * BLK, [[TPC, 128], [1, BLK]]),
                              waits=wrv)
                wos = p.consume(f"osb{b}")

                def addfn(v, b=b):
                    return v.tensor_add(aff(OSB[b], BLK), aff(REV[b], BLK), aff(TEV[b], BLK))
                _, ad = p.step("vector", "ve", addfn, waits=[("ac", evv), (f"lx{b}", rl)] + wos)
                p.mark(f"tev{b}", "ve", ad)
                p.mark(f"rev{b}", "ve", ad)
                _, st = p.dma("gpsimd", f"st{b}",
                              dap(x3_park, oc * 128 * BLK, [[BLK, 128], [1, BLK]]),
                              aff(OSB[b], BLK), waits=[("ve", ad)])
                p.mark(f"osb{b}", f"st{b}", st)
                wo_log[oc] = (f"st{b}", st)
                wxb = p.consume(f"fxb{b}")
                _, scv = p.step("vector", "ve",
                                lambda v, b=b: v.tensor_copy(abf(XBB[b], BLK), aff(OSB[b], BLK)),
                                waits=wxb)
                p.mark(f"osb{b}", "ve", scv)
                wxq = p.consume(f"fxq{b}")
                _, ssq = p.step("scalar", "ac",
                                lambda s, b=b: s.activation(abf(XSQ[b], BLK), aff(OSB[b], BLK), AF.Square),
                                waits=[("ve", ad)] + wxq)
                p.mark(f"osb{b}", "ac", ssq)

                def stmm(t, b=b, oc=oc):
                    t.matmul(ps_a[0:1, 0:BLK], onesbc[:], abf(XBB[b], BLK),
                             start=(oc == 0), stop=(oc == NCH - 1))
                    return t.matmul(ps_a[0:1, BLK:2 * BLK], onesbc[:], abf(XSQ[b], BLK),
                                    start=(oc == 0), stop=(oc == NCH - 1))
                _, smt = p.step("tensor", "te", stmm, waits=[("ve", scv), ("ac", ssq)])
                p.mark(f"fxb{b}", "te", smt)
                p.mark(f"fxq{b}", "te", smt)
                wo_stat[0] = smt
            return p.st_totals()

        for qi in range(2 if _LVL >= 99 else (1 if _LVL >= 6 else 0)):
            ao_ready = emit_attention(qi)
            wo_log = {}
            wo_stat = [0]
            x3_done = emit_wo(qi, ao_ready, wo_log, wo_stat)
            xn3 = emit_ln((x3_park, 0, BLK), XN3,
                          chunk_waits=(lambda ci: [wo_log[2 * ci], wo_log[2 * ci + 1]]),
                          pre=(0, BLK, wo_stat[0]))
            cr3 = emit_router(XN3, CR3, xn3)
            emit_ffn([
                dict(xoff=XN3, croff=CR3, xn_ready=xn3, cr_ready=cr3,
                     rsrc=(x3_park, 0, BLK), dst=(out_fm, qi * BLK, TPC, None)),
            ], wmode="cached")
            if debug == "x3" and qi == 0:
                p.dma("gpsimd", "st0", dbg.ap(), x3_park.ap(), waits=x3_done)
                break

        if debug == "y1":
            p.dma("gpsimd", "st0", dbg.ap(), y1_park.ap(), waits=sty_all)
        elif debug == "ln2":
            for ti in range(2):
                for chi in range(NCH):
                    p.dma("gpsimd", "st0",
                          dap(dbg, chi * 128 * TPC + ti * BLK, [[TPC, 128], [1, BLK]]),
                          abf(XNA + ti * NCH * BLK + chi * BLK, BLK),
                          waits=[("ve", xn2[1])])
        elif debug == "kv":
            p.dma("gpsimd", "st0", dbg.ap(), kv_own.ap(), waits=stk_all)
        elif debug == "q":
            p.dma("gpsimd", "st0", dbg.ap(), q_park.ap(), waits=stq_all)

        fin = p.st_totals()

        def finwait(g):
            for s, v in fin:
                g.wait_ge(p.hdl(s), v)
        p.add("gpsimd", finwait)

        with nc.Block() as block:
            p.replay(block)
    return nc


def _tile_oc(wt, oc_cnt):
    icc = wt.shape[0] // 128
    return np.ascontiguousarray(
        wt.reshape(icc, 128, oc_cnt, 128).transpose(2, 1, 0, 3))


def _consts():
    dm = np.zeros((128, 4 * BLK), np.float32)
    for kc in range(4):
        for pp in range(128):
            dm[pp, kc * BLK + kc * 128 + pp:kc * BLK + BLK] = 1.0
    mask2 = np.zeros((16, 1024), np.float32)
    for m in range(1024):
        mask2[m >> 6, m] = 0.5
    return dm.astype(BF), mask2.astype(BF)


def _tab_for_core(c):
    tab = np.zeros((128, 66), np.float32)
    tab[:, 64] = 1.0
    tab[:, 65] = EXP_BIAS
    for qi, gblk in ((0, c), (1, 15 - c)):
        for kb in range(16):
            idx = qi * 16 + kb
            if kb < gblk:
                tab[:, 2 * idx] = 1.0
                tab[:, 2 * idx + 1] = EXP_BIAS
            else:
                tab[:, 2 * idx] = 0.0
                tab[:, 2 * idx + 1] = -200.0
    return tab


def kernel(L0_attn_x, L0_attn_Wq, L0_attn_Wk, L0_attn_Wv, L0_attn_Wo,
           L0_ffn_base_w, L0_ffn_base_b, L0_ffn_A, L0_ffn_B, L0_ffn_router_w,
           _debug=None):
    x = np.asarray(L0_attn_x, np.float32).reshape(S, D)
    xT = np.ascontiguousarray(x.T)
    wq4 = _tile_oc(np.ascontiguousarray(np.asarray(L0_attn_Wq, np.float32).T), NCH)
    wk4 = _tile_oc(np.ascontiguousarray(np.asarray(L0_attn_Wk, np.float32).T), 8)
    wv4 = _tile_oc(np.ascontiguousarray(np.asarray(L0_attn_Wv, np.float32).T), 8)
    wo4 = _tile_oc(np.ascontiguousarray(np.asarray(L0_attn_Wo, np.float32).T), NCH)
    bw4 = _tile_oc(np.ascontiguousarray(np.asarray(L0_ffn_base_w, np.float32).T), NCH)
    b2 = np.ascontiguousarray(
        np.asarray(L0_ffn_B, np.float32).transpose(0, 2, 1).reshape(16 * 64, D))
    b24 = _tile_oc(b2, NCH)
    art = np.concatenate([np.asarray(L0_ffn_A, np.float32).T,
                          np.asarray(L0_ffn_router_w, np.float32).T], axis=1)
    art4 = np.ascontiguousarray(
        art.reshape(NCH, 128, 80).transpose(1, 0, 2).reshape(128, NCH * 80))
    bias_t = np.ascontiguousarray(np.asarray(L0_ffn_base_b, np.float32).reshape(NCH, 128).T)
    dmc, mask2c = _consts()
    ident = np.eye(128, dtype=np.float32)
    onesr = np.ones((1, 128), np.float32)
    onesc = np.ones((128, 1), np.float32)
    onesbc = np.ones((128, 1), np.float32).astype(BF)

    nc = build_program(debug=_debug)
    in_maps = []
    for c in range(NCORE):
        x_l = np.ascontiguousarray(np.concatenate(
            [xT[:, c * BLK:(c + 1) * BLK], xT[:, (15 - c) * BLK:(16 - c) * BLK]], axis=1))
        in_maps.append(dict(
            x_loc=x_l, wq4=wq4, wk4=wk4, wv4=wv4, wo4=wo4, bw4=bw4, b24=b24,
            art4=art4, bias_t=bias_t, tab_t=_tab_for_core(c), dm_t=dmc,
            mask2_t=mask2c, ident_t=ident, onesr_t=onesr, onesc_t=onesc,
            onesbc_t=onesbc))
    res = run_bass_kernel_spmd(nc, in_maps, core_ids=list(range(NCORE)))

    if _debug is not None:
        return res

    outT = np.empty((D, S), np.float32)
    for c in range(NCORE):
        sh = np.asarray(res.results[c]["out_fm"], np.float32)
        outT[:, c * BLK:(c + 1) * BLK] = sh[:, :BLK]
        outT[:, (15 - c) * BLK:(16 - c) * BLK] = sh[:, BLK:]
    return np.ascontiguousarray(outT.T).reshape(1, S, D)


if __name__ == "__main__":
    nc = build_program()
    n = sum(len(v) for v in nc.__dict__.get("_dummy", {}) or []) if False else None
    print("build ok")


# version 18 (final)
# speedup vs baseline: 1.0245x; 1.0147x over previous
import os
import sys
sys.path.insert(0, "/opt/trn_rl_repo")
from contextlib import ExitStack

RT_TRUNC = float(os.environ.get("RT_TRUNC", "9"))

import numpy as np
import ml_dtypes

import concourse.bass as bass
import concourse.mybir as mybir
from concourse.bass_utils import run_bass_kernel_spmd

F32 = mybir.dt.float32
BF16 = mybir.dt.bfloat16
AF = mybir.ActivationFunctionType
ALU = mybir.AluOpType
AX = mybir.AxisListType
BF = ml_dtypes.bfloat16

D = 4096
S = 8192
NCORE = 8
BLK = 512
TPC = 1024
NKV = 8
GQ = 4
HD = 128
KVD = NKV * HD
NCH = D // 128
QSCALE = float(1.0 / np.sqrt(HD))
EXP_BIAS = -16.0
LN_EPS = 1e-5
NVIS = 17


class P:
    def __init__(self, nc, ctx):
        self.nc = nc
        self.q = {k: [] for k in ("sync", "tensor", "scalar", "vector", "gpsimd")}
        self.sems = {}
        self.res = {}
        self.ctx = ctx

    def sem(self, name):
        self.sems[name] = [self.ctx.enter_context(self.nc.semaphore(name)), 0]

    def tgt(self, name, n=1):
        s = self.sems[name]
        s[1] += n
        return s[0], s[1]

    def cnt(self, name):
        return self.sems[name][1]

    def hdl(self, name):
        return self.sems[name][0]

    def add(self, eng, fn):
        self.q[eng].append(fn)

    def consume(self, rname):
        w = self.res.get(rname, [])
        self.res[rname] = []
        return list(w)

    def mark(self, rname, sem, val):
        self.res.setdefault(rname, []).append((sem, val))

    def dma(self, qeng, sem, out_ap, in_ap, waits=(), wres=None, transpose=False):
        allw = list(waits)
        if wres:
            for rn in wres:
                allw += self.consume(rn)
        h, t = self.tgt(sem, 16)

        def f(E, out_ap=out_ap, in_ap=in_ap, allw=tuple(allw), h=h, tr=transpose):
            for ws, wv in allw:
                E.wait_ge(self.hdl(ws), wv)
            E.dma_start(out=out_ap, in_=in_ap, transpose=tr).then_inc(h, 16)

        self.add(qeng, f)
        return sem, t

    def step(self, eng, sem, body, waits=(), wres=None, n=1):
        allw = list(waits)
        if wres:
            for rn in wres:
                allw += self.consume(rn)
        h, t = self.tgt(sem, n)

        def f(E, body=body, allw=tuple(allw), h=h, n=n):
            for ws, wv in allw:
                E.wait_ge(self.hdl(ws), wv)
            body(E).then_inc(h, n)

        self.add(eng, f)
        return sem, t

    def entry(self, *sems):
        return [(s, self.cnt(s)) for s in sems if self.cnt(s) > 0]

    def st_totals(self):
        return self.entry("st0", "st1")

    def replay(self, block):
        for eng in ("sync", "tensor", "scalar", "vector", "gpsimd"):
            fns = self.q[eng]
            getattr(block, eng)(lambda E, fns=fns: [f(E) for f in fns])


def dap(t, off, dims):
    return bass.AP(t, off, dims)


def build_program(debug=None):
    nc = bass.Bass("TRN2", target_bir_lowering=False)

    x_loc = nc.dram_tensor("x_loc", [D, TPC], F32, kind="ExternalInput")
    wq4 = nc.dram_tensor("wq4", [NCH, 128, NCH, 128], F32, kind="ExternalInput")
    wk4 = nc.dram_tensor("wk4", [8, 128, NCH, 128], F32, kind="ExternalInput")
    wv4 = nc.dram_tensor("wv4", [8, 128, NCH, 128], F32, kind="ExternalInput")
    wo4 = nc.dram_tensor("wo4", [NCH, 128, NCH, 128], F32, kind="ExternalInput")
    bw4 = nc.dram_tensor("bw4", [NCH, 128, NCH, 128], F32, kind="ExternalInput")
    b24 = nc.dram_tensor("b24", [NCH, 128, 8, 128], F32, kind="ExternalInput")
    art4 = nc.dram_tensor("art4", [128, NCH * 80], F32, kind="ExternalInput")
    bias_t = nc.dram_tensor("bias_t", [128, NCH], F32, kind="ExternalInput")
    tab_t = nc.dram_tensor("tab_t", [128, 66], F32, kind="ExternalInput")
    dm_t = nc.dram_tensor("dm_t", [128, 4 * BLK], BF16, kind="ExternalInput")
    mask2_t = nc.dram_tensor("mask2_t", [16, 1024], BF16, kind="ExternalInput")
    ident_t = nc.dram_tensor("ident_t", [128, 128], F32, kind="ExternalInput")
    onesr_t = nc.dram_tensor("onesr_t", [1, 128], F32, kind="ExternalInput")
    onesc_t = nc.dram_tensor("onesc_t", [128, 1], F32, kind="ExternalInput")
    onesbc_t = nc.dram_tensor("onesbc_t", [128, 1], BF16, kind="ExternalInput")

    out_fm = nc.dram_tensor("out_fm", [D, TPC], F32, kind="ExternalOutput")

    kv_own = nc.dram_tensor("kv_own", [2 * KVD, TPC], BF16)
    kv_all = nc.dram_tensor("kv_all", [NCORE * 2 * KVD, TPC], BF16, addr_space="Shared")
    y1_park = nc.dram_tensor("y1_park", [D, TPC], F32)
    q_park = nc.dram_tensor("q_park", [D, TPC], BF16)
    x3_park = nc.dram_tensor("x3_park", [D, BLK], F32)
    rt_lg = nc.dram_tensor("rt_lg", [16, BLK], BF16)
    bwc = nc.dram_tensor("bwc", [NCH, 128, 40 * 128], BF16)
    woc = nc.dram_tensor("woc", [NCH, 128, NCH * 128], BF16)
    rt_cmb = nc.dram_tensor("rt_cmb", [4 * 128, 128], BF16)

    dbg = None
    if debug in ("y1",):
        dbg = nc.dram_tensor("dbg", [D, TPC], F32, kind="ExternalOutput")
    elif debug in ("ln1", "ln2"):
        dbg = nc.dram_tensor("dbg", [D, TPC], BF16, kind="ExternalOutput")
    elif debug == "kv":
        dbg = nc.dram_tensor("dbg", [2 * KVD, TPC], BF16, kind="ExternalOutput")
    elif debug == "q":
        dbg = nc.dram_tensor("dbg", [D, TPC], BF16, kind="ExternalOutput")
    elif debug == "x3":
        dbg = nc.dram_tensor("dbg", [D, BLK], F32, kind="ExternalOutput")
    elif debug == "cr":
        dbg = nc.dram_tensor("dbg", [128, 8 * BLK], BF16, kind="ExternalOutput")

    _LVL = {"ln1": 1, "cr": 1.5, "y1": 2, "ln2": 3, "kv": 4, "q": 5, "x3": 6}.get(debug, 99)

    with ExitStack() as ctx:
        e = ctx.enter_context
        p = P(nc, ctx)
        for s in ("ldc", "lw0", "lw1", "lx0", "lx1", "lk0", "lk1",
                  "lv0", "lv1", "lv2", "lv3", "lk2", "lk3", "lq", "st0", "st1", "cc", "rts", "rtl", "wc",
                  "te", "ac", "ve", "gp"):
            p.sem(s)

        ab = e(nc.sbuf_tensor("ab", [128, 58880], BF16))
        af = e(nc.sbuf_tensor("af", [128, 21504], F32))
        tab = e(nc.sbuf_tensor("tab", [128, 66], F32))
        dm = e(nc.sbuf_tensor("dm", [128, 4 * BLK], BF16))
        mask2 = e(nc.sbuf_tensor("mask2", [16, 1024], BF16))
        ident = e(nc.sbuf_tensor("ident", [128, 128], F32))
        onesr = e(nc.sbuf_tensor("onesr", [1, 128], F32))
        onesc = e(nc.sbuf_tensor("onesc", [128, 1], F32))
        onesbc = e(nc.sbuf_tensor("onesbc", [128, 1], BF16))
        biasb = e(nc.sbuf_tensor("biasb", [128, NCH], F32))
        lgsb = e(nc.sbuf_tensor("lgsb", [16, BLK], BF16))
        lgq = e(nc.sbuf_tensor("lgq", [128, 64], BF16))
        cmb16 = e(nc.sbuf_tensor("cmb16", [128, 64], BF16))

        WBF = [0, 5120]
        XNA = 10240
        ARTB = 43008
        CRA = 45568
        CR3 = 45568
        XBB = [53760, 54784]
        XSQ = [55808, 56832]
        KVQ = [57856, 58368]
        QSB = 0
        AOSB = 16384
        PSB = [32768, 34816, 36864, 38912]
        KSB = [40960, 41472, 41984, 42496]
        VTB = [49664, 50176, 50688, 51200]
        XN3 = 16384
        CBQ = 57856
        WST = [0, 5120]
        XCH = [10240, 12288]
        TEV = [14336, 14848]
        REV = [15360, 15872]
        OSB = [16384, 16896]
        RES2 = 17408
        RTR = 17920
        LNR = 18432
        RACC = [10240, 12288]
        BREC = 14336

        ps_a = e(nc.psum_tensor("ps_a", [128, 2048], F32))
        ps_b = e(nc.psum_tensor("ps_b", [128, 2048], F32))

        def abf(off, n):
            return ab[:, off:off + n]

        def aff(off, n):
            return af[:, off:off + n]

        def row(off, n):
            return af[0:1, off:off + n]

        p.dma("sync", "ldc", tab[:], tab_t[:, :])
        p.dma("sync", "ldc", dm[:], dm_t[:, :])
        p.dma("sync", "ldc", mask2[:], mask2_t[:, :])
        p.dma("sync", "ldc", ident[:], ident_t[:, :])
        p.dma("sync", "ldc", onesr[:], onesr_t[:, :])
        p.dma("sync", "ldc", onesc[:], onesc_t[:, :])
        p.dma("sync", "ldc", onesbc[:], onesbc_t[:, :])
        _, cld = p.dma("sync", "ldc", biasb[:], bias_t[:, :])
        p.step("vector", "ve", lambda v: v.tensor_scalar_mul(biasb[:], biasb[:], 0.5),
               waits=[("ldc", cld)])
        _, artld = p.dma("sync", "lw0", aff(WST[0], 2560), art4.ap())
        _, artcv = p.step("vector", "ve",
                          lambda v: v.tensor_copy(abf(ARTB, 2560), aff(WST[0], 2560)),
                          waits=[("lw0", artld)])
        p.mark("wst0", "ve", artcv)

        def emit_ln(src, xoff, extra_waits=(), chunk_waits=None):
            tsr, coff, pitch = src
            ssum = ps_a[0:1, 0:BLK]
            qsum = ps_a[0:1, BLK:2 * BLK]
            aps = ps_a[:, 2 * BLK:3 * BLK]
            bps = ps_a[:, 3 * BLK:4 * BLK]
            ew = p.entry("ve", "ac", "gp")

            def chunk2(ci, buf, extra):
                _, t16 = p.dma("sync", f"lx{buf}", aff(XCH[buf], 2 * BLK),
                               dap(tsr, (2 * ci) * 128 * pitch + coff,
                                   [[pitch, 128], [128 * pitch, 2], [1, BLK]]),
                               waits=extra, wres=[f"xch{buf}"])
                return aff(XCH[buf], 2 * BLK), [(f"lx{buf}", t16)]

            for ci in range(NCH // 2):
                buf = ci % 2
                cw = chunk_waits(ci) if chunk_waits else (list(extra_waits) if ci < 2 else [])
                ap_in, w = chunk2(ci, buf, cw)
                _, cv = p.step("vector", "ve",
                               lambda v, a=ap_in, b=buf: v.tensor_copy(abf(XBB[b], 2 * BLK), a),
                               waits=w, wres=[f"xbb{buf}"])
                _, sq = p.step("scalar", "ac",
                               lambda s, a=ap_in, b=buf: s.activation(abf(XSQ[b], 2 * BLK), a, AF.Square),
                               waits=w, wres=[f"xsq{buf}"])

                def mmfn(t, b=buf, ci=ci):
                    last = None
                    for k in range(2):
                        t.matmul(ssum, onesbc[:], abf(XBB[b] + k * BLK, BLK),
                                 start=(ci == 0 and k == 0), stop=(ci == NCH // 2 - 1 and k == 1))
                        last = t.matmul(qsum, onesbc[:], abf(XSQ[b] + k * BLK, BLK),
                                        start=(ci == 0 and k == 0), stop=(ci == NCH // 2 - 1 and k == 1))
                    return last
                _, mt = p.step("tensor", "te", mmfn,
                               waits=[("ve", cv), ("ac", sq)]
                                     + ((ew + [("ldc", p.cnt("ldc"))]) if ci == 0 else []))
                p.mark(f"xbb{buf}", "te", mt)
                p.mark(f"xsq{buf}", "te", mt)
                p.mark(f"xch{buf}", "te", mt)
            te_stats = p.cnt("te")
            mean_r, ex2_r, var_r, sd_r, al_r, be_r = (row(LNR + i * BLK, BLK) for i in range(6))

            wlnr = p.consume("lnr")
            _, s1_ = p.step("vector", "ve",
                            lambda v: v.tensor_scalar_mul(mean_r, ssum, 1.0 / D),
                            waits=[("te", te_stats)] + wlnr)
            _, s2_ = p.step("vector", "ve",
                            lambda v: v.tensor_scalar_mul(ex2_r, qsum, 1.0 / D),
                            waits=[("ve", s1_)])
            _, s3_ = p.step("vector", "ve",
                            lambda v: v.tensor_tensor(var_r, mean_r, mean_r, op=ALU.mult),
                            waits=[("ve", s2_)])
            _, s4_ = p.step("vector", "ve",
                            lambda v: v.tensor_sub(var_r, ex2_r, var_r),
                            waits=[("ve", s3_)])
            _, fv = p.step("vector", "ve",
                           lambda v: v.tensor_scalar_add(var_r, var_r, LN_EPS),
                           waits=[("ve", s4_)])
            _, fs = p.step("scalar", "ac", lambda s: s.activation(sd_r, var_r, AF.Sqrt),
                           waits=[("ve", fv)])
            _, r1_ = p.step("vector", "ve", lambda v: v.reciprocal(al_r, sd_r),
                            waits=[("ac", fs)])
            _, r2_ = p.step("vector", "ve",
                            lambda v: v.tensor_tensor(be_r, mean_r, al_r, op=ALU.mult),
                            waits=[("ve", r1_)])
            _, fv2 = p.step("vector", "ve",
                            lambda v: v.tensor_scalar_mul(be_r, be_r, -1.0),
                            waits=[("ve", r2_)])

            def bcfn(t):
                t.matmul(aps, onesr[:], al_r, start=True, stop=True)
                return t.matmul(bps, onesr[:], be_r, start=True, stop=True)
            _, bct = p.step("tensor", "te", bcfn, waits=[("ve", fv2)])
            p.mark("lnr", "te", bct)

            aps2 = dap(ps_a, 2 * BLK * 128 // 128 * 0 + 2 * BLK, [[2048, 128], [0, 2], [1, BLK]])
            bps2 = dap(ps_a, 3 * BLK, [[2048, 128], [0, 2], [1, BLK]])
            for ci in range(NCH // 2):
                buf = ci % 2
                ap_in, w = chunk2(ci, buf, ())
                wtv = p.consume(f"tevln{buf}")

                def appfn(v, a=ap_in, b=buf, ci=ci):
                    v.tensor_tensor(dap(af, XCH[b], [[21504, 128], [BLK, 2], [1, BLK]]) if False else a,
                                    a, aps2, op=ALU.mult)
                    return v.tensor_tensor(abf(xoff + 2 * ci * BLK, 2 * BLK), a, bps2, op=ALU.add)
                _, av = p.step("vector", "ve", appfn,
                               waits=w + [("te", bct)] + wtv)
                p.mark(f"xch{buf}", "ve", av)
            return p.cnt("ve")

        def emit_router(xoff, croff, xn_ready):
            arp = ps_b[0:80, 2 * BLK:3 * BLK]
            c2p = [ps_b[:, 2 * BLK:3 * BLK], ps_b[:, 3 * BLK:4 * BLK]]
            ew = p.entry("ve", "ac", "gp")
            if RT_TRUNC < 1:
                return p.cnt("ve")

            def armm(t):
                last = None
                for ic in range(NCH):
                    last = t.matmul(arp, abf(ARTB + ic * 80, 80),
                                    abf(xoff + ic * BLK, BLK),
                                    start=(ic == 0), stop=(ic == NCH - 1))
                return last
            _, art_t = p.step("tensor", "te", armm, waits=[("ve", xn_ready)] + ew)
            if RT_TRUNC < 1.5:
                return p.cnt("ve")

            res2_lo = af[0:64, RES2:RES2 + BLK]
            res2_hi = af[64:128, RES2:RES2 + BLK]

            def evfn(s):
                s.copy(lgsb[:], ps_b[64:80, 2 * BLK:3 * BLK])
                s.copy(res2_lo, ps_b[0:64, 2 * BLK:3 * BLK])
                return s.copy(res2_hi, ps_b[0:64, 2 * BLK:3 * BLK])
            _, ev = p.step("scalar", "ac", evfn, waits=[("te", art_t)],
                           wres=["res2", "lgsb"])
            if RT_TRUNC < 2:
                return p.cnt("ve")
            _, lgst = p.dma("gpsimd", "rts", rt_lg.ap(), lgsb[:],
                            waits=[("ac", ev)], wres=["rt_lg"])
            p.mark("lgsb", "rts", lgst)
            lql = None
            wlq = p.consume("lgq")
            for j in range(4):
                _, lql = p.dma("sync", "rtl", lgq[:, j * 16:(j + 1) * 16],
                               dap(rt_lg, j * 128, [[BLK, 16], [1, 128]]),
                               waits=[("rts", lgst)] + (wlq if j == 0 else []),
                               transpose=True)
            p.mark("rt_lg", "rtl", lql)
            if RT_TRUNC < 3:
                return p.cnt("ve")
            _, sm = p.step("scalar", "ac",
                           lambda s: s.activation(aff(RTR, 64), lgq[:], AF.Exp),
                           waits=[("rtl", lql)], wres=["rtr"])
            p.mark("lgq", "ac", sm)
            if RT_TRUNC < 4:
                return p.cnt("ve")

            g3 = lambda o: dap(af, RTR + o, [[21504, 128], [16, 4], [1, 16]])
            g1 = lambda o: dap(af, RTR + o, [[21504, 128], [1, 4]])
            g1b = lambda o: dap(af, RTR + o, [[21504, 128], [1, 4], [0, 16]])
            chain = [
                lambda v: v.tensor_reduce(g1(192), g3(0), axis=AX.X, op=ALU.add),
                lambda v: v.reciprocal(g1(196), g1(192)),
                lambda v: v.tensor_tensor(g3(0), g3(0), g1b(196), op=ALU.mult),
                lambda v: v.tensor_reduce(g1(200), g3(0), axis=AX.X, op=ALU.max),
                lambda v: v.tensor_tensor(g3(64), g3(0), g1b(200), op=ALU.is_ge),
                lambda v: v.tensor_tensor(g3(128), g3(0), g3(64), op=ALU.mult),
                lambda v: v.tensor_sub(g3(128), g3(0), g3(128)),
                lambda v: v.tensor_reduce(g1(204), g3(128), axis=AX.X, op=ALU.max),
                lambda v: v.tensor_tensor(g3(128), g3(128), g1b(204), op=ALU.is_ge),
                lambda v: v.tensor_tensor(g3(64), g3(64), g3(128), op=ALU.max),
                lambda v: v.tensor_tensor(g3(0), g3(0), g3(64), op=ALU.mult),
                lambda v: v.tensor_tensor(g1(200), g1(200), g1(204), op=ALU.add),
                lambda v: v.tensor_single_scalar(g1(200), g1(200), 1e-6, op=ALU.add),
                lambda v: v.reciprocal(g1(196), g1(200)),
                lambda v: v.tensor_tensor(g3(0), g3(0), g1b(196), op=ALU.mult),
            ]
            prev = [("ac", sm)]
            for fn in chain:
                _, t2 = p.step("vector", "ve", fn, waits=prev)
                prev = [("ve", t2)]
            wcm = p.consume("cmb16")
            _, cmv = p.step("vector", "ve",
                            lambda v: v.tensor_copy(cmb16[:], aff(RTR, 64)),
                            waits=prev + wcm)
            p.mark("rtr", "ve", cmv)
            if RT_TRUNC < 5:
                return p.cnt("ve")
            cst = None
            wrc = p.consume("rt_cmb")
            for g in range(4):
                _, cst = p.dma("gpsimd", "rts",
                               dap(rt_cmb, g * 128 * 128, [[128, 128], [1, 16]]),
                               cmb16[:, g * 16:(g + 1) * 16],
                               waits=[("ve", cmv)] + (wrc if g == 0 else []))
            p.mark("cmb16", "rts", cst)
            cbl = None
            wcb = p.consume("cbq") + p.consume("kvq0")
            for g in range(4):
                _, cbl = p.dma("sync", "rtl", abf(CBQ + g * 128, 128),
                               dap(rt_cmb, g * 128 * 128, [[128, 128], [1, 128]]),
                               waits=[("rts", cst)] + ((wcb + [("te", p.cnt("te"))]) if g == 0 else []),
                               transpose=True)
            p.mark("rt_cmb", "rtl", cbl)
            if RT_TRUNC < 6:
                return p.cnt("ve")
            cr_done = p.cnt("ve")
            for er in range(8):
                b = er % 2
                wr = p.consume(f"c2p{b}")

                def cbmm(t, er=er, b=b):
                    return t.matmul(c2p[b], mask2[:, er * 128:(er + 1) * 128],
                                    ab[0:16, CBQ:CBQ + BLK], start=True, stop=True)
                _, cbt = p.step("tensor", "te", cbmm, waits=[("rtl", cbl)] + wr)
                if er == 7:
                    p.mark("cbq", "te", cbt)
                    p.mark("kvq0", "te", cbt)

                def crfn(v, er=er, b=b):
                    return v.tensor_tensor(abf(croff + er * BLK, BLK), aff(RES2, BLK),
                                           c2p[b], op=ALU.mult)
                _, cr_done = p.step("vector", "ve", crfn, waits=[("te", cbt)])
                p.mark(f"c2p{b}", "ve", cr_done)
                p.mark("res2", "ve", cr_done) if er == 7 else None
            return cr_done

        def stream_w(oc, src4, wb, extra=None, cache=None):
            wr = p.consume(f"wst{wb}")
            sem = f"lw{wb}"
            _, l1 = p.dma("sync", sem, aff(WST[wb], NCH * 128),
                          dap(src4, oc * 128 * D, [[D, 128], [1, NCH * 128]]),
                          waits=wr)
            tot = NCH * 128
            if extra is not None:
                src4b, icntb, dstoff = extra
                _, l1 = p.dma("sync", sem, aff(WST[wb] + dstoff, icntb * 128),
                              dap(src4b, oc * 128 * (icntb * 128), [[icntb * 128, 128], [1, icntb * 128]]),
                              waits=wr)
                tot = dstoff + icntb * 128
            wrb = p.consume(f"wbf{wb}")
            half = (tot // 2) // 128 * 128
            _, cv = p.step("gpsimd", "gp",
                           lambda g, wb=wb, half=half: g.tensor_copy(
                               abf(WBF[wb], half), aff(WST[wb], half)),
                           waits=[(sem, l1)] + wrb)
            _, cv2 = p.step("vector", "ve",
                            lambda v, wb=wb, half=half, tot=tot: v.tensor_copy(
                                abf(WBF[wb] + half, tot - half), aff(WST[wb] + half, tot - half)),
                            waits=[(sem, l1)] + wrb)
            p.mark(f"wst{wb}", "gp", cv)
            p.mark(f"wst{wb}", "ve", cv2)
            if cache is not None:
                ct, ctot = cache
                _, cs = p.dma("gpsimd", "wc", dap(ct, oc * 128 * ctot, [[ctot, 128], [1, ctot]]),
                              abf(WBF[wb], ctot), waits=[("gp", cv), ("ve", cv2)])
                p.mark(f"wbf{wb}", "wc", cs)
            return [("gp", cv), ("ve", cv2)]

        def stream_wc(oc, csrc, wb, tot):
            wrb = p.consume(f"wbf{wb}")
            sem = f"lw{wb}"
            _, l1 = p.dma("sync", sem, abf(WBF[wb], tot),
                          dap(csrc, oc * 128 * tot, [[tot, 128], [1, tot]]),
                          waits=wrb + [("wc", p.cnt("wc"))])
            return [(sem, l1)]

        def emit_ffn(tiles, wmode="stream", pre_cv=None):
            fps = [ps_b[:, 0:BLK], ps_b[:, BLK:2 * BLK]]
            ew = p.entry("ve", "ac", "gp")
            for oc in range(NCH):
                wb = oc % 2
                if pre_cv is not None and oc < len(pre_cv):
                    cv = pre_cv[oc]
                elif wmode == "cache":
                    cv = stream_w(oc, bw4, wb, extra=(b24, 8, NCH * 128),
                                  cache=(bwc, 40 * 128))
                elif wmode == "cached":
                    cv = stream_wc(oc, bwc, wb, 40 * 128)
                else:
                    cv = stream_w(oc, bw4, wb, extra=(b24, 8, NCH * 128))
                for ti, tl in enumerate(tiles):
                    b = (oc * len(tiles) + ti) % 2
                    wfp = p.consume(f"fps{b}")

                    def mmfn(t, wb=wb, b=b, tl=tl):
                        last = None
                        for ic in range(NCH):
                            t.matmul(fps[b], abf(WBF[wb] + ic * 128, 128),
                                     abf(tl["xoff"] + ic * BLK, BLK),
                                     start=(ic == 0), stop=False)
                        for er in range(8):
                            last = t.matmul(fps[b], abf(WBF[wb] + NCH * 128 + er * 128, 128),
                                            abf(tl["croff"] + er * BLK, BLK),
                                            start=False, stop=(er == 7))
                        return last
                    _, mt = p.step("tensor", "te", mmfn,
                                   waits=cv + [("ve", tl["xn_ready"]),
                                               ("ve", tl["cr_ready"])] + wfp
                                         + (ew if oc == 0 and ti == 0 else []))
                    p.mark(f"wbf{wb}", "te", mt)
                    wtv = p.consume(f"tev{b}")
                    _, evv = p.step("vector", "ve",
                                    lambda v, b=b, oc=oc: v.tensor_scalar(
                                        aff(TEV[b], BLK), fps[b], 0.5,
                                        biasb[:, oc:oc + 1], op0=ALU.mult, op1=ALU.add),
                                    waits=[("te", mt)] + wtv)
                    p.mark(f"fps{b}", "ve", evv)
                    rsrc = tl["rsrc"]
                    wrv = p.consume(f"rev{b}")
                    _, rl = p.dma("sync", f"lx{b}", aff(REV[b], BLK),
                                  dap(rsrc[0], oc * 128 * rsrc[2] + rsrc[1], [[rsrc[2], 128], [1, BLK]]),
                                  waits=wrv)
                    wos = p.consume(f"osb{b}")

                    def addfn(v, b=b):
                        return v.tensor_add(aff(OSB[b], BLK), aff(REV[b], BLK), aff(TEV[b], BLK))
                    _, ad = p.step("vector", "ve", addfn,
                                   waits=[(f"lx{b}", rl)] + wos)
                    p.mark(f"tev{b}", "ve", ad)
                    p.mark(f"rev{b}", "ve", ad)
                    dst = tl["dst"]
                    _, st = p.dma("gpsimd", f"st{b}",
                                  dap(dst[0], oc * 128 * dst[2] + dst[1], [[dst[2], 128], [1, BLK]]),
                                  aff(OSB[b], BLK), waits=[("ve", ad)])
                    p.mark(f"osb{b}", f"st{b}", st)
                    if "log" in tl:
                        tl["log"][oc] = (f"st{b}", st)

        def emit_proj(src4, noc, xn_rdy2, scale, dst, dsem_unused, rowoff):
            fps = [ps_b[:, 0:BLK], ps_b[:, BLK:2 * BLK]]
            ew = p.entry("ve", "ac", "gp")
            for oc in range(noc):
                wb = oc % 2
                cv = stream_w(oc, src4, wb)
                for ti in range(2):
                    b = (oc * 2 + ti) % 2
                    wfp = p.consume(f"fps{b}")

                    def mmfn(t, wb=wb, b=b, ti=ti):
                        last = None
                        for ic in range(NCH):
                            last = t.matmul(fps[b], abf(WBF[wb] + ic * 128, 128),
                                            abf(XNA + ti * NCH * BLK + ic * BLK, BLK),
                                            start=(ic == 0), stop=(ic == NCH - 1))
                        return last
                    _, mt = p.step("tensor", "te", mmfn,
                                   waits=cv + [("ve", xn_rdy2[ti])] + wfp
                                         + (ew if oc == 0 and ti == 0 else []))
                    p.mark(f"wbf{wb}", "te", mt)
                    wkv = p.consume(f"kvq{b}")
                    if scale == 1.0:
                        _, evv = p.step("vector", "ve",
                                        lambda v, b=b: v.tensor_copy(abf(KVQ[b], BLK), fps[b]),
                                        waits=[("te", mt)] + wkv)
                    else:
                        _, evv = p.step("vector", "ve",
                                        lambda v, b=b: v.tensor_scalar_mul(abf(KVQ[b], BLK),
                                                                           fps[b], scale),
                                        waits=[("te", mt)] + wkv)
                    p.mark(f"fps{b}", "ve", evv)
                    _, st = p.dma("gpsimd", f"st{b}",
                                  dap(dst, (rowoff + oc * 128) * TPC + ti * BLK, [[TPC, 128], [1, BLK]]),
                                  abf(KVQ[b], BLK), waits=[("ve", evv)])
                    p.mark(f"kvq{b}", f"st{b}", st)

        xn1 = [emit_ln((x_loc, ti * BLK, TPC), XNA + ti * NCH * BLK) for ti in range(2)]
        if debug == "ln1":
            for ti in range(2):
                for chi in range(NCH):
                    p.dma("gpsimd", "st0",
                          dap(dbg, chi * 128 * TPC + ti * BLK, [[TPC, 128], [1, BLK]]),
                          abf(XNA + ti * NCH * BLK + chi * BLK, BLK),
                          waits=[("ve", xn1[1])])
        if _LVL >= 1.5:
            cr1 = [emit_router(XNA + ti * NCH * BLK, CRA + ti * 8 * BLK, xn1[ti]) for ti in range(2)]
        if debug == "cr":
            p.dma("gpsimd", "st0", dbg.ap(), abf(CRA, 8 * BLK),
                  waits=[("ve", cr1[0]), ("ve", cr1[1])])
        ffn1_logs = [{}, {}]
        if _LVL >= 2:
            emit_ffn([
                dict(xoff=XNA, croff=CRA, xn_ready=xn1[0], cr_ready=cr1[0],
                     rsrc=(x_loc, 0, TPC), dst=(y1_park, 0, TPC, None), log=ffn1_logs[0]),
                dict(xoff=XNA + NCH * BLK, croff=CRA + 8 * BLK, xn_ready=xn1[1], cr_ready=cr1[1],
                     rsrc=(x_loc, BLK, TPC), dst=(y1_park, BLK, TPC, None), log=ffn1_logs[1]),
            ], wmode="cache")
        sty_all = p.st_totals()
        if _LVL >= 3:
            xn2 = [emit_ln((y1_park, ti * BLK, TPC), XNA + ti * NCH * BLK,
                           chunk_waits=(lambda ci, ti=ti: [ffn1_logs[ti][2 * ci],
                                                           ffn1_logs[ti][2 * ci + 1]]))
                   for ti in range(2)]
            if debug == "ln2":
                for ti in range(2):
                    for chi in range(NCH):
                        p.dma("gpsimd", "st0",
                              dap(dbg, chi * 128 * TPC + ti * BLK, [[TPC, 128], [1, BLK]]),
                              abf(XNA + ti * NCH * BLK + chi * BLK, BLK),
                              waits=[("ve", xn2[1])])
        if _LVL >= 4:
            emit_proj(wk4, 8, xn2, 1.0, kv_own, "stk", 0)
            emit_proj(wv4, 8, xn2, 1.0, kv_own, "stk", KVD)
        stk_all = p.st_totals()
        if _LVL >= 5:
            _, ccv = p.step("gpsimd", "cc",
                            lambda g: g.collective_compute(
                                "AllGather", ALU.bypass,
                                replica_groups=[list(range(NCORE))],
                                ins=[kv_own.ap()], outs=[kv_all.ap()]),
                            waits=stk_all)
            emit_proj(wq4, NCH, xn2, QSCALE, q_park, "stq", 0)
        stq_all = p.st_totals()

        def emit_attention(qi):
            S_ps = ps_a
            AV_ps = ps_b
            nvis = 9 if qi == 0 else 17
            ewq = p.entry("te", "gp", "ve", "ac")
            qld = None
            for gd in range(4):
                _, qld = p.dma("sync", "lq", abf(QSB + gd * 4096, 4096),
                               dap(q_park, gd * 8 * 128 * TPC + qi * BLK,
                                   [[TPC, 128], [128 * TPC, 8], [1, BLK]]),
                               waits=stq_all + [("cc", ccv)] + (ewq if gd == 0 else []),
                               wres=["qsb"] if gd == 0 else None)
            rst = None
            evv = None
            for h in range(NKV):
                wrg = p.consume("raccg") + p.entry("te", "ve", "ac")
                _, msg = p.step("gpsimd", "gp",
                                lambda g: g.memset(aff(RACC[0], 4 * BLK), 0.0), waits=wrg)
                wrv = p.consume("raccv") + p.entry("te", "ac", "gp")
                _, msv = p.step("vector", "ve",
                                lambda v: v.memset(aff(RACC[1], 4 * BLK), 0.0), waits=wrv)
                gk = 0
                av_last = None
                pends = []
                for v in range(nvis):
                    par = v % 4
                    if v < nvis - 1:
                        kb = v
                        own = kb if kb < 8 else 15 - kb
                        slot = 0 if kb < 8 else 1
                        kvt = kv_all
                        krow = own * 2 * KVD + h * 128
                        vrow = own * 2 * KVD + KVD + h * 128
                        kcol = slot * BLK
                    else:
                        kvt = kv_own
                        krow = h * 128
                        vrow = KVD + h * 128
                        kcol = qi * BLK
                    _, kl = p.dma("sync", f"lk{par}", abf(KSB[par], BLK),
                                  dap(kvt, krow * TPC + kcol, [[TPC, 128], [1, BLK]]),
                                  wres=[f"ksb{par}"])
                    vl = None
                    wvs = p.consume(f"vtb{par}")
                    for kc in range(4):
                        _, vl = p.dma("sync", f"lv{par}", abf(VTB[par] + kc * 128, 128),
                                      dap(kvt, vrow * TPC + kcol + kc * 128, [[TPC, 128], [1, 128]]),
                                      waits=wvs if kc == 0 else (), transpose=True)
                    if v < nvis - 1:
                        idx = qi * 16 + kb
                        sc_ap = tab[:, 2 * idx:2 * idx + 1]
                        bi_ap = tab[:, 2 * idx + 1:2 * idx + 2]
                    else:
                        sc_ap = tab[:, 64:65]
                        bi_ap = tab[:, 65:66]
                    for kc in range(4):
                        pp = gk % 4
                        for hf in range(2):
                            ho = hf * 2
                            hoff = hf * 2 * BLK

                            def smm(t, par=par, kc=kc, h=h, ho=ho, hoff=hoff):
                                last = None
                                for g in (ho, ho + 1):
                                    last = t.matmul(S_ps[:, g * BLK:(g + 1) * BLK],
                                                    abf(KSB[par] + kc * 128, 128),
                                                    abf(QSB + (h * GQ + g) * BLK, BLK),
                                                    start=True, stop=True)
                                return last
                            sw = [(f"lk{par}", kl)] + p.consume(f"s_ps{hf}")
                            if v == 0 and h == 0 and kc == 0 and hf == 0:
                                sw.append(("lq", qld))
                            _, st_ = p.step("tensor", "te", smm, waits=sw)
                            if kc == 3 and hf == 1:
                                p.mark(f"ksb{par}", "te", st_)
                            if len(pends) >= 2:
                                av_last = pends.pop(0)()
                            wps = p.consume(f"psb{pp}_{hf}")

                            def expfn(s, pp=pp, hf=hf, hoff=hoff, sc=sc_ap, bi=bi_ap):
                                return s.activation(abf(PSB[pp] + hoff, 2 * BLK),
                                                    S_ps[:, hoff:hoff + 2 * BLK], AF.Exp,
                                                    bias=bi, scale=sc)
                            _, ex = p.step("scalar", "ac", expfn, waits=[("te", st_)] + wps)
                            p.mark(f"s_ps{hf}", "ac", ex)
                            pready = [("ac", ex)]
                            if v == nvis - 1:
                                def mskfn(vv, pp=pp, hoff=hoff, kc=kc):
                                    return vv.tensor_tensor(
                                        dap(ab, PSB[pp] + hoff, [[58880, 128], [BLK, 2], [1, BLK]]),
                                        dap(ab, PSB[pp] + hoff, [[58880, 128], [BLK, 2], [1, BLK]]),
                                        dap(dm, kc * BLK, [[4 * BLK, 128], [0, 2], [1, BLK]]),
                                        op=ALU.mult)
                                _, mk = p.step("vector", "ve", mskfn, waits=pready)
                                pready = [("ve", mk)]
                            if hf == 0:
                                exp_both = []
                            exp_both = exp_both + pready
                            if hf == 1:
                                on_gp = (gk % 5) < 2
                                eng = "gpsimd" if on_gp else "vector"
                                srw = "gp" if on_gp else "ve"
                                racc_o = RACC[0 if on_gp else 1]
                                waits_r = exp_both + pready[-1:] if False else exp_both + pready
                                if gk == 0:
                                    waits_r = waits_r + [("gp", msg)]
                                if gk == 2:
                                    waits_r = waits_r + [("ve", msv)]

                                def rafn(E, pp=pp, racc_o=racc_o):
                                    return E.tensor_tensor(
                                        aff(racc_o, 4 * BLK), aff(racc_o, 4 * BLK),
                                        abf(PSB[pp], 4 * BLK), op=ALU.add)
                                _, ra = p.step(eng, srw, rafn, waits=waits_r)
                                p.mark(f"psb{pp}_0", srw, ra)
                                p.mark(f"psb{pp}_1", srw, ra)

                            def mk_av(pready=pready, vl=vl, pp=pp, par=par, kc=kc, ho=ho,
                                      hf=hf, hoff=hoff,
                                      first=(v == 0 and kc == 0), last=(v == nvis - 1 and kc == 3)):
                                def avmm(t):
                                    li = None
                                    for g in (ho, ho + 1):
                                        li = t.matmul(AV_ps[:, g * BLK:(g + 1) * BLK],
                                                      abf(VTB[par] + kc * 128, 128),
                                                      abf(PSB[pp] + g * BLK, BLK),
                                                      start=first, stop=last)
                                    return li
                                aw = pready + [(f"lv{par}", vl)] + p.consume(f"av_ps{hf}")
                                _, avt = p.step("tensor", "te", avmm, waits=aw)
                                p.mark(f"psb{pp}_{hf}", "te", avt)
                                if kc == 3:
                                    p.mark(f"vtb{par}", "te", avt)
                                return avt
                            pends.append(mk_av)
                        gk += 1
                while pends:
                    av_last = pends.pop(0)()
                def rsmm(t):
                    last = None
                    for g in range(GQ):
                        t.matmul(S_ps[0:1, g * BLK:(g + 1) * BLK], onesc[:],
                                 aff(RACC[0] + g * BLK, BLK), start=True, stop=False)
                        last = t.matmul(S_ps[0:1, g * BLK:(g + 1) * BLK], onesc[:],
                                        aff(RACC[1] + g * BLK, BLK), start=False, stop=True)
                    return last
                _, rst = p.step("tensor", "te", rsmm,
                                waits=[("gp", p.cnt("gp")), ("ve", p.cnt("ve")),
                                       ("ac", p.cnt("ac"))] + p.consume("s_ps0") + p.consume("s_ps1"))
                p.mark("raccg", "te", rst)
                p.mark("raccv", "te", rst)
                _, rc = p.step("vector", "ve",
                               lambda v2: v2.reciprocal(row(LNR, 4 * BLK), S_ps[0:1, :]),
                               waits=[("te", rst)])

                def bcmm(t):
                    last = None
                    for g in range(GQ):
                        last = t.matmul(S_ps[:, g * BLK:(g + 1) * BLK], onesr[:],
                                        row(LNR + g * BLK, BLK), start=True, stop=True)
                    return last
                _, bct = p.step("tensor", "te", bcmm, waits=[("ve", rc)])
                wbr = p.consume("brec")
                _, bcv = p.step("scalar", "ac",
                                lambda s2: s2.copy(aff(BREC, 2048), ps_a[:, :]),
                                waits=[("te", bct)] + wbr)
                p.mark("s_ps0", "ac", bcv)
                p.mark("s_ps1", "ac", bcv)

                def evfn(v2, h=h):
                    for g in range(GQ - 1):
                        v2.tensor_tensor(abf(AOSB + (h * GQ + g) * BLK, BLK),
                                         AV_ps[:, g * BLK:(g + 1) * BLK],
                                         aff(BREC + g * BLK, BLK), op=ALU.mult)
                    return v2.tensor_tensor(abf(AOSB + (h * GQ + 3) * BLK, BLK),
                                            AV_ps[:, 3 * BLK:4 * BLK],
                                            aff(BREC + 3 * BLK, BLK), op=ALU.mult)
                _, evv = p.step("vector", "ve", evfn, waits=[("ac", bcv), ("te", av_last)])
                p.mark("av_ps0", "ve", evv)
                p.mark("av_ps1", "ve", evv)
                p.mark("brec", "ve", evv)
            for rn in ("xch0", "xch1", "tev0", "tev1", "rev0", "rev1", "osb0", "osb1"):
                p.mark(rn, "te", p.cnt("te"))
                p.mark(rn, "ve", p.cnt("ve"))
                p.mark(rn, "gp", p.cnt("gp"))
                p.mark(rn, "ac", p.cnt("ac"))
            p.mark("res2", "ve", p.cnt("ve"))
            p.mark("rtr", "ve", p.cnt("ve"))
            p.mark("lnr", "ve", p.cnt("ve"))
            for rn in ("wbf0", "wbf1"):
                p.mark(rn, "te", p.cnt("te"))
            p.mark("qsb", "te", p.cnt("te"))
            return p.cnt("ve")

        def emit_wo(qi, ao_ready, wo_log):
            wops = [ps_b[:, 0:BLK], ps_b[:, BLK:2 * BLK]]
            ew = p.entry("ve", "ac", "gp")
            for oc in range(NCH):
                wb = oc % 2
                if qi == 0:
                    cv = stream_w(oc, wo4, wb, cache=(woc, NCH * 128))
                else:
                    cv = stream_wc(oc, woc, wb, NCH * 128)
                b = oc % 2
                wfp = p.consume(f"fps{b}")

                def mmfn(t, wb=wb, b=b):
                    last = None
                    for ic in range(NCH):
                        last = t.matmul(wops[b], abf(WBF[wb] + ic * 128, 128),
                                        abf(AOSB + ic * BLK, BLK),
                                        start=(ic == 0), stop=(ic == NCH - 1))
                    return last
                _, mt = p.step("tensor", "te", mmfn,
                               waits=cv + [("ve", ao_ready)] + wfp
                                     + (ew if oc == 0 else []))
                p.mark(f"wbf{wb}", "te", mt)
                wtv = p.consume(f"tev{b}")
                _, evv = p.step("scalar", "ac",
                                lambda s, b=b: s.copy(aff(TEV[b], BLK), wops[b]),
                                waits=[("te", mt)] + wtv)
                p.mark(f"fps{b}", "ac", evv)
                wrv = p.consume(f"rev{b}")
                _, rl = p.dma("sync", f"lx{b}", aff(REV[b], BLK),
                              dap(y1_park, oc * 128 * TPC + qi * BLK, [[TPC, 128], [1, BLK]]),
                              waits=wrv)
                wos = p.consume(f"osb{b}")

                def addfn(v, b=b):
                    return v.tensor_add(aff(OSB[b], BLK), aff(REV[b], BLK), aff(TEV[b], BLK))
                _, ad = p.step("vector", "ve", addfn, waits=[("ac", evv), (f"lx{b}", rl)] + wos)
                p.mark(f"tev{b}", "ve", ad)
                p.mark(f"rev{b}", "ve", ad)
                _, st = p.dma("gpsimd", f"st{b}",
                              dap(x3_park, oc * 128 * BLK, [[BLK, 128], [1, BLK]]),
                              aff(OSB[b], BLK), waits=[("ve", ad)])
                p.mark(f"osb{b}", f"st{b}", st)
                wo_log[oc] = (f"st{b}", st)
            return p.st_totals()

        for qi in range(2 if _LVL >= 99 else (1 if _LVL >= 6 else 0)):
            ao_ready = emit_attention(qi)
            wo_log = {}
            x3_done = emit_wo(qi, ao_ready, wo_log)
            xn3 = emit_ln((x3_park, 0, BLK), XN3,
                          chunk_waits=(lambda ci: [wo_log[2 * ci], wo_log[2 * ci + 1]]))
            cr3 = emit_router(XN3, CR3, xn3)
            emit_ffn([
                dict(xoff=XN3, croff=CR3, xn_ready=xn3, cr_ready=cr3,
                     rsrc=(x3_park, 0, BLK), dst=(out_fm, qi * BLK, TPC, None)),
            ], wmode="cached")
            if debug == "x3" and qi == 0:
                p.dma("gpsimd", "st0", dbg.ap(), x3_park.ap(), waits=x3_done)
                break

        if debug == "y1":
            p.dma("gpsimd", "st0", dbg.ap(), y1_park.ap(), waits=sty_all)
        elif debug == "ln2":
            for ti in range(2):
                for chi in range(NCH):
                    p.dma("gpsimd", "st0",
                          dap(dbg, chi * 128 * TPC + ti * BLK, [[TPC, 128], [1, BLK]]),
                          abf(XNA + ti * NCH * BLK + chi * BLK, BLK),
                          waits=[("ve", xn2[1])])
        elif debug == "kv":
            p.dma("gpsimd", "st0", dbg.ap(), kv_own.ap(), waits=stk_all)
        elif debug == "q":
            p.dma("gpsimd", "st0", dbg.ap(), q_park.ap(), waits=stq_all)

        fin = p.st_totals()

        def finwait(g):
            for s, v in fin:
                g.wait_ge(p.hdl(s), v)
        p.add("gpsimd", finwait)

        with nc.Block() as block:
            p.replay(block)
    return nc


def _tile_oc(wt, oc_cnt):
    icc = wt.shape[0] // 128
    return np.ascontiguousarray(
        wt.reshape(icc, 128, oc_cnt, 128).transpose(2, 1, 0, 3))


def _consts():
    dm = np.zeros((128, 4 * BLK), np.float32)
    for kc in range(4):
        for pp in range(128):
            dm[pp, kc * BLK + kc * 128 + pp:kc * BLK + BLK] = 1.0
    mask2 = np.zeros((16, 1024), np.float32)
    for m in range(1024):
        mask2[m >> 6, m] = 0.5
    return dm.astype(BF), mask2.astype(BF)


def _tab_for_core(c):
    tab = np.zeros((128, 66), np.float32)
    tab[:, 64] = 1.0
    tab[:, 65] = EXP_BIAS
    for qi, gblk in ((0, c), (1, 15 - c)):
        for kb in range(16):
            idx = qi * 16 + kb
            if kb < gblk:
                tab[:, 2 * idx] = 1.0
                tab[:, 2 * idx + 1] = EXP_BIAS
            else:
                tab[:, 2 * idx] = 0.0
                tab[:, 2 * idx + 1] = -200.0
    return tab


def kernel(L0_attn_x, L0_attn_Wq, L0_attn_Wk, L0_attn_Wv, L0_attn_Wo,
           L0_ffn_base_w, L0_ffn_base_b, L0_ffn_A, L0_ffn_B, L0_ffn_router_w,
           _debug=None):
    x = np.asarray(L0_attn_x, np.float32).reshape(S, D)
    xT = np.ascontiguousarray(x.T)
    wq4 = _tile_oc(np.ascontiguousarray(np.asarray(L0_attn_Wq, np.float32).T), NCH)
    wk4 = _tile_oc(np.ascontiguousarray(np.asarray(L0_attn_Wk, np.float32).T), 8)
    wv4 = _tile_oc(np.ascontiguousarray(np.asarray(L0_attn_Wv, np.float32).T), 8)
    wo4 = _tile_oc(np.ascontiguousarray(np.asarray(L0_attn_Wo, np.float32).T), NCH)
    bw4 = _tile_oc(np.ascontiguousarray(np.asarray(L0_ffn_base_w, np.float32).T), NCH)
    b2 = np.ascontiguousarray(
        np.asarray(L0_ffn_B, np.float32).transpose(0, 2, 1).reshape(16 * 64, D))
    b24 = _tile_oc(b2, NCH)
    art = np.concatenate([np.asarray(L0_ffn_A, np.float32).T,
                          np.asarray(L0_ffn_router_w, np.float32).T], axis=1)
    art4 = np.ascontiguousarray(
        art.reshape(NCH, 128, 80).transpose(1, 0, 2).reshape(128, NCH * 80))
    bias_t = np.ascontiguousarray(np.asarray(L0_ffn_base_b, np.float32).reshape(NCH, 128).T)
    dmc, mask2c = _consts()
    ident = np.eye(128, dtype=np.float32)
    onesr = np.ones((1, 128), np.float32)
    onesc = np.ones((128, 1), np.float32)
    onesbc = np.ones((128, 1), np.float32).astype(BF)

    nc = build_program(debug=_debug)
    in_maps = []
    for c in range(NCORE):
        x_l = np.ascontiguousarray(np.concatenate(
            [xT[:, c * BLK:(c + 1) * BLK], xT[:, (15 - c) * BLK:(16 - c) * BLK]], axis=1))
        in_maps.append(dict(
            x_loc=x_l, wq4=wq4, wk4=wk4, wv4=wv4, wo4=wo4, bw4=bw4, b24=b24,
            art4=art4, bias_t=bias_t, tab_t=_tab_for_core(c), dm_t=dmc,
            mask2_t=mask2c, ident_t=ident, onesr_t=onesr, onesc_t=onesc,
            onesbc_t=onesbc))
    res = run_bass_kernel_spmd(nc, in_maps, core_ids=list(range(NCORE)))

    if _debug is not None:
        return res

    outT = np.empty((D, S), np.float32)
    for c in range(NCORE):
        sh = np.asarray(res.results[c]["out_fm"], np.float32)
        outT[:, c * BLK:(c + 1) * BLK] = sh[:, :BLK]
        outT[:, (15 - c) * BLK:(16 - c) * BLK] = sh[:, BLK:]
    return np.ascontiguousarray(outT.T).reshape(1, S, D)


if __name__ == "__main__":
    nc = build_program()
    n = sum(len(v) for v in nc.__dict__.get("_dummy", {}) or []) if False else None
    print("build ok")
